# version 08
# baseline (speedup 1.0000x reference)
import numpy as np
from contextlib import ExitStack

import concourse.bass as bass
import concourse.mybir as mybir
from concourse.bass_utils import run_bass_kernel_spmd
from concourse.alu_op_type import AluOpType as OP

F32 = mybir.dt.float32
BF16 = mybir.dt.bfloat16
AF = mybir.ActivationFunctionType
AX = mybir.AxisListType.X
PI = float(np.pi)
PI_HI = float(np.float32(np.pi))
PI_LO = float(np.float32(np.pi - np.float64(np.float32(np.pi))))
PI2_HI = float(np.float32(np.pi / 2))
PI2_LO = float(np.float32(np.pi / 2 - np.float64(np.float32(np.pi / 2))))
C02_HI = float(np.float32(0.2))
C02_LO = float(np.float32(0.2 - np.float64(np.float32(0.2))))
C01_HI = float(np.float32(0.1))
C01_LO = float(np.float32(0.1 - np.float64(np.float32(0.1))))

N_CORES = 8
N_SUB_FULL = 32
_STAGE = "full"

_A_LIT = np.array(
    [[1, 1, 1, 0, 1, 1, 1, 0, 0, 0, 1, 1], [1, 1, 0, 1, 1, 1, 0, 0, 0, 1, 1, 0],
     [1, 0, 1, 1, 1, 0, 0, 0, 1, 1, 0, 1], [0, 1, 1, 1, 0, 0, 0, 1, 1, 0, 1, 1],
     [0, 0, 0, 0, 1, 1, 1, 1, 1, 1, 1, 1], [1, 1, 1, 1, 0, 0, 0, 0, 0, 0, 0, 0],
     [0, 0, 0, 0, 1, 1, 1, 1, 0, 0, 0, 0], [1, 1, 0, 0, 1, 1, 0, 0, 1, 1, 0, 0],
     [1, 0, 1, 0, 1, 0, 1, 0, 1, 0, 1, 0], [0, 1, 0, 1, 0, 1, 0, 1, 0, 1, 0, 1],
     [1, 0, 0, 1, 1, 0, 0, 1, 1, 0, 0, 1], [0, 1, 1, 0, 0, 1, 1, 0, 0, 1, 1, 0]],
    dtype=np.float32)

_CONSTS = None


def _consts():
    global _CONSTS
    if _CONSTS is not None:
        return _CONSTS
    n = 1024
    k = np.fft.fftfreq(n)
    ka = np.abs(k) + 1e-8
    kaF = np.ascontiguousarray(
        ka.reshape(8, 128).T).astype(np.float32)
    d = np.arange(n)
    ang = 2.0 * np.pi * np.outer(d, d) / n
    cosb = np.cos(ang)
    sinb = np.sin(ang)
    coshi = cosb.astype(np.float32)
    coslo = (cosb - coshi).astype(np.float32)
    sinhi = sinb.astype(np.float32)
    sinlo = (sinb - sinhi).astype(np.float32)
    H = np.concatenate([_A_LIT.T, np.eye(12, dtype=np.float32)], axis=1)
    Ht = np.ascontiguousarray(H.T)
    eye = np.eye(128, dtype=np.float32)
    cvec = np.array([[-10.0, -5.0, 0.0, 5.0, 10.0]], dtype=np.float32)
    try:
        import ctypes
        _lm = ctypes.CDLL("libm.so.6")
        _lm.sinf.restype = ctypes.c_float
        _lm.sinf.argtypes = [ctypes.c_float]
        _lm.cosf.restype = ctypes.c_float
        _lm.cosf.argtypes = [ctypes.c_float]
        _sinf = lambda a: np.array([_lm.sinf(float(t)) for t in a], np.float32)
        _cosf = lambda a: np.array([_lm.cosf(float(t)) for t in a], np.float32)
    except (OSError, AttributeError):
        _sinf = lambda a: np.sin(a.astype(np.float64)).astype(np.float32)
        _cosf = lambda a: np.cos(a.astype(np.float64)).astype(np.float32)
    f32 = np.float32
    ka32 = (np.abs(np.fft.fftfreq(n).astype(f32)) + f32(1e-8)).astype(f32)
    D1 = f32(1.0)
    om1 = (f32(1.0) + f32(0.2) * D1).astype(f32)
    a1 = (f32(1.0) + f32(0.5) * D1).astype(f32)
    b1 = (f32(0.1) * D1).astype(f32)
    arg1 = (om1 * D1 + a1 * ka32).astype(f32)
    ph1 = (om1 * D1 - ka32 + b1 * (ka32 * ka32)).astype(f32)
    re1 = _sinf(arg1)
    frc = ((re1 * _cosf(ph1)).astype(f32).astype(np.float64) / n).astype(f32)
    fic = ((re1 * _sinf(ph1)).astype(f32).astype(np.float64) / n).astype(f32)
    filtc = np.ascontiguousarray(np.concatenate(
        [frc.reshape(8, 128).T, fic.reshape(8, 128).T], axis=1))
    _CONSTS = dict(kaF=kaF, coshi=coshi, coslo=coslo, sinhi=sinhi, sinlo=sinlo,
                   Ht=Ht, eye=eye, cvec=cvec, filtc=filtc)
    return _CONSTS


class _Sched:

    def __init__(self, sems):
        self.ev = {}
        self.cnt = {}
        self.emit_eng = None
        self.eng = None
        self.sems = sems
        self.waited = {}

    def begin(self, emit_eng, eng):
        self.emit_eng = emit_eng
        self.eng = eng
        self.waited = {}

    def op(self, etag, fn, inc=None):
        if self.emit_eng is None:
            if inc is not None:
                sem_name = "s_" + etag
                c = self.cnt.get(sem_name, 0) + 1
                self.cnt[sem_name] = c
                self.ev[inc] = (sem_name, c)
            return
        if etag != self.emit_eng:
            return
        instr = fn(self.eng)
        if inc is not None:
            sem_name, _ = self.ev[inc]
            instr.then_inc(self.sems[sem_name], 1)

    def alias(self, key, src_key):
        if self.emit_eng is None:
            self.ev[key] = self.ev[src_key]

    def wait(self, etag, key):
        if self.emit_eng is None or etag != self.emit_eng:
            return
        sem_name, cntv = self.ev[key]
        if self.waited.get(sem_name, -1) >= cntv:
            return
        self.eng.wait_ge(self.sems[sem_name], cntv)
        self.waited[sem_name] = cntv

    def wait_dma(self, etag, sem_name, value):
        if self.emit_eng is None or etag != self.emit_eng:
            return
        if self.waited.get(sem_name, -1) >= value:
            return
        self.eng.wait_ge(self.sems[sem_name], value)
        self.waited[sem_name] = value


def _build(n_sub=N_SUB_FULL):
    nc = bass.Bass("TRN2", target_bir_lowering=False,
                   detect_race_conditions=False)
    n_tok = n_sub * 128

    dx = nc.dram_tensor("x", [n_tok, 1024], F32, kind="ExternalInput").ap()
    dtok0 = nc.dram_tensor("tok0", [1, 1024], F32, kind="ExternalInput").ap()
    dwel = nc.dram_tensor("wel", [1024, 24], F32, kind="ExternalInput").ap()
    dbel = nc.dram_tensor("bel", [1, 24], F32, kind="ExternalInput").ap()
    dwle = nc.dram_tensor("wle", [24, 1024], F32, kind="ExternalInput").ap()
    dble = nc.dram_tensor("ble", [1, 1024], F32, kind="ExternalInput").ap()
    dw1 = nc.dram_tensor("w1r", [1024, 4096], F32, kind="ExternalInput").ap()
    db1 = nc.dram_tensor("b1r", [1, 4096], F32, kind="ExternalInput").ap()
    dw2 = nc.dram_tensor("w2r", [4096, 1024], F32, kind="ExternalInput").ap()
    db2 = nc.dram_tensor("b2r", [8, 1024], F32, kind="ExternalInput").ap()
    dcwr = nc.dram_tensor("cwr", [1024, 16], F32, kind="ExternalInput").ap()
    dcent = nc.dram_tensor("cent", [8, 1024], F32, kind="ExternalInput").ap()
    dka = nc.dram_tensor("kaF", [128, 8], F32, kind="ExternalInput").ap()
    dfiltc = nc.dram_tensor("filtc", [128, 16], F32, kind="ExternalInput").ap()
    dht = nc.dram_tensor("Ht", [24, 12], F32, kind="ExternalInput").ap()
    deye = nc.dram_tensor("eye", [128, 128], F32, kind="ExternalInput").ap()
    dcvec = nc.dram_tensor("cvec", [1, 5], F32, kind="ExternalInput").ap()
    dcoshi = nc.dram_tensor("coshi", [1024, 1024], F32, kind="ExternalInput").ap()
    dcoslo = nc.dram_tensor("coslo", [1024, 1024], F32, kind="ExternalInput").ap()
    dsinhi = nc.dram_tensor("sinhi", [1024, 1024], F32, kind="ExternalInput").ap()
    dsinlo = nc.dram_tensor("sinlo", [1024, 1024], F32, kind="ExternalInput").ap()
    do = nc.dram_tensor("o", [n_tok, 1024], F32, kind="ExternalOutput").ap()

    with ExitStack() as ctx:
        e = ctx.enter_context
        W1bf = e(nc.sbuf_tensor("W1bf", [128, 32768], BF16))
        W2bf = e(nc.sbuf_tensor("W2bf", [128, 32768], BF16))
        Tring = e(nc.sbuf_tensor("Tring", [128, 1408], BF16))
        HFs = [e(nc.sbuf_tensor(f"HFs{j}", [128, 4096], BF16)) for j in range(2)]
        bfA = [e(nc.sbuf_tensor(f"bfA{j}", [128, 1024], BF16)) for j in range(2)]
        bfF = [e(nc.sbuf_tensor(f"bfF{j}", [128, 1024], BF16)) for j in range(2)]
        x0b = [e(nc.sbuf_tensor(f"x0b{j}", [128, 1024], F32)) for j in range(2)]
        x2b = [e(nc.sbuf_tensor(f"x2b{j}", [128, 1024], F32)) for j in range(2)]
        x3b = [e(nc.sbuf_tensor(f"x3b{j}", [128, 1024], F32)) for j in range(2)]
        Htmp = [e(nc.sbuf_tensor(f"Htmp{j}", [128, 512], BF16)) for j in range(4)]
        gdiag = e(nc.sbuf_tensor("gdiag", [128, 1024], BF16))
        CWbf = e(nc.sbuf_tensor("CWbf", [128, 192], BF16))
        CWr = e(nc.sbuf_tensor("CWr", [128, 128], BF16))
        WelBF = e(nc.sbuf_tensor("WelBF", [128, 192], BF16))
        WleBF = e(nc.sbuf_tensor("WleBF", [24, 1024], BF16))
        Htbf = e(nc.sbuf_tensor("Htbf", [24, 12], BF16))
        b1bf = e(nc.sbuf_tensor("b1bf", [1, 4096], BF16))
        b2bf = e(nc.sbuf_tensor("b2bf", [8, 1024], BF16))
        belbf = e(nc.sbuf_tensor("belbf", [1, 24], BF16))
        blebf = e(nc.sbuf_tensor("blebf", [1, 1024], BF16))
        eyeb = e(nc.sbuf_tensor("eyeb", [128, 128], BF16))
        eyef = e(nc.sbuf_tensor("eyef", [8, 8], F32))
        onesb = e(nc.sbuf_tensor("onesb", [1, 128], BF16))
        onesf = e(nc.sbuf_tensor("onesf", [1, 128], F32))
        kaS = e(nc.sbuf_tensor("kaS", [128, 8], F32))
        tok0S = e(nc.sbuf_tensor("tok0S", [1, 1024], F32))
        cvecS = e(nc.sbuf_tensor("cvecS", [1, 5], F32))
        cnts = e(nc.sbuf_tensor("cnts", [1, 5], F32))
        lyS = e(nc.sbuf_tensor("lyS", [1, 5], F32))
        scr5 = e(nc.sbuf_tensor("scr5", [1, 5], F32))
        smI = e(nc.sbuf_tensor("smI", [1, 16], F32))
        stat12 = e(nc.sbuf_tensor("stat12", [128, 12], F32))
        mv = e(nc.sbuf_tensor("mv", [128, 2], F32))
        sm1 = e(nc.sbuf_tensor("sm1", [128, 14], F32))
        filt = e(nc.sbuf_tensor("filt", [128, 80], F32))
        bitsf = e(nc.sbuf_tensor("bitsf", [128, 24], F32))
        bitsb = e(nc.sbuf_tensor("bitsb", [128, 24], BF16))
        corrf = e(nc.sbuf_tensor("corrf", [128, 24], F32))
        corrb = e(nc.sbuf_tensor("corrb", [128, 24], BF16))
        syn01 = e(nc.sbuf_tensor("syn01", [128, 12], F32))
        g8 = [e(nc.sbuf_tensor(f"g8_{j}", [128, 8], F32)) for j in range(2)]
        g8b = [e(nc.sbuf_tensor(f"g8b{j}", [128, 8], BF16))
               for j in range(2)]
        gFbf = [e(nc.sbuf_tensor(f"gFbf{j}", [8, 128], BF16)) for j in range(2)]
        tFbf = e(nc.sbuf_tensor("tFbf", [24, 128], BF16))
        rscp = e(nc.sbuf_tensor("rscp", [8, 4], F32))
        rrow = e(nc.sbuf_tensor("rrow", [1, 8], F32))
        rscB = e(nc.sbuf_tensor("rscB", [128, 8], F32))
        lgt = e(nc.sbuf_tensor("lgt", [128, 49], F32))
        kst = e(nc.sbuf_tensor("kst", [128, 3], F32))
        gsc = e(nc.sbuf_tensor("gsc", [128, 16], F32))

        t0n = x3b[0]
        TTbuf = HFs[1]
        hr2b = HFs[0]
        hrv2b_ofs = 2048
        scrR = x3b[1]
        centS = x2b[1]
        scrC = x2b[0]
        pA0 = e(nc.psum_tensor("pA0", [128, 1024], F32))
        pA1 = e(nc.psum_tensor("pA1", [128, 1024], F32))
        pH = e(nc.psum_tensor("pH", [128, 1024], F32))
        pT = e(nc.psum_tensor("pT", [128, 2048], BF16))
        pA = [pA0, pA1]

        sem_names = ["s_t", "s_v", "s_s", "s_g", "s_x0", "s_x1", "s_w1", "s_w2",
                     "s_sm", "s_b0", "s_b1", "s_tr", "s_tt", "s_st0", "s_st1"]
        sems = {nm: e(nc.semaphore(nm)) for nm in sem_names}
        block = e(nc.Block())

        S = _Sched(sems)
        N_SM = 29 * 16

        base_srcs = []
        for kc in range(8):
            for src in (dcoshi, dcoslo, dsinhi, dsinlo):
                base_srcs.append((src, kc))

        ptb = {"n": 0}

        def pt_batch(titems, evac_fn, t_alias=None, s_alias=None, rhs=None,
                     evac_eng="s"):
            b = ptb["n"]
            ptb["n"] += 1
            ptb.setdefault("eng", {})[b] = evac_eng
            slot = (b % 2) * 1024
            if b >= 2:
                S.wait("t", (ptb["eng"][b - 2], "ptevac", b - 2))

            def tfn(t):
                last = None
                for co, in_ap in titems:
                    width = in_ap.shape[0]
                    if rhs is None:
                        last = t.transpose(
                            pT[0:in_ap.shape[1], slot + co:slot + co + width],
                            in_ap, eyeb[:, :])
                    else:
                        last = t.matmul(
                            pA1[0:in_ap.shape[1],
                                slot // 2 + co:slot // 2 + co + width],
                            in_ap, rhs, start=True, stop=True)
                return last
            S.op("t", tfn, inc=("t", "ptb", b))
            if t_alias is not None:
                S.alias(t_alias, ("t", "ptb", b))
            S.wait(evac_eng, ("t", "ptb", b))
            S.op(evac_eng, lambda e: evac_fn(e, slot),
                 inc=(evac_eng, "ptevac", b))
            if s_alias is not None:
                S.alias(s_alias, (evac_eng, "ptevac", b))

        def walk():
            ptb["n"] = 0
            s1guard = set()
            def g_memsets(g):
                g.memset(onesb[:, :], 1.0)
                g.memset(onesf[:, :], 1.0)
                g.memset(smI[:, 0:1], 2.0)
                g.memset(kst[:, 0:1], 1e-5)
                g.memset(kst[:, 1:2], PI)
                g.memset(gdiag[:, :], 0.0)
                return g.memset(kst[:, 2:3], PI / 2)
            S.op("g", g_memsets, inc=("g", "consts"))

            def g_small(g):
                sm = sems["s_sm"]
                g.dma_start(tok0S[:, :], dtok0).then_inc(sm, 16)
                g.dma_start(kaS[:, :], dka).then_inc(sm, 16)
                g.dma_start(x2b[0][:, 0:16], dfiltc).then_inc(sm, 16)
                g.dma_start(eyef[:, :], deye[0:8, 0:8]).then_inc(sm, 16)
                g.dma_start(eyeb[:, :], deye).then_inc(sm, 16)
                g.dma_start(cvecS[:, :], dcvec).then_inc(sm, 16)
                g.dma_start(Htbf[:, :], dht).then_inc(sm, 16)
                g.dma_start(WleBF[:, :], dwle).then_inc(sm, 16)
                g.dma_start(belbf[:, :], dbel).then_inc(sm, 16)
                g.dma_start(blebf[:, :], dble).then_inc(sm, 16)
                g.dma_start(b1bf[:, :], db1).then_inc(sm, 16)
                g.dma_start(b2bf[:, :], db2).then_inc(sm, 16)
                g.dma_start(centS[0:8, :], dcent).then_inc(sm, 16)
                for kc in range(8):
                    g.dma_start(WelBF[:, kc * 24:(kc + 1) * 24],
                                dwel[kc * 128:(kc + 1) * 128, :]).then_inc(sm, 16)
                for kc in range(8):
                    g.dma_start(CWr[:, kc * 16:(kc + 1) * 16],
                                dcwr[kc * 128:(kc + 1) * 128, :]).then_inc(sm, 16)
            S.op("g", lambda g: g_small(g))

            S.op("y", lambda y: y.dma_start(
                x0b[0][:, :], dx[0:128, :]).then_inc(sems["s_x0"], 16))
            if n_sub > 1:
                S.op("y", lambda y: y.dma_start(
                    x0b[1][:, :], dx[128:256, :]).then_inc(sems["s_x1"], 16))

            S.wait_dma("v", "s_sm", N_SM)
            S.wait("v", ("g", "consts"))

            def v_t0stats(v):
                v.bn_stats(stat12[0:1, 0:6], tok0S[:, 0:512])
                v.bn_stats(stat12[0:1, 6:12], tok0S[:, 512:1024])
                v.drain()
                return v.bn_aggr(smI[0:1, 1:3], stat12[0:1, 0:12])
            S.op("v", v_t0stats, inc=("v", "t0stats"))
            S.wait("s", ("v", "t0stats"))
            S.wait("s", ("g", "consts"))

            def s_rstd0(s):
                s.activation(smI[0:1, 13:14], smI[0:1, 2:3], AF.Ln, bias=kst[0:1, 0:1])
                s.drain()
                return s.activation(smI[0:1, 3:4], smI[0:1, 13:14], AF.Exp, scale=-0.5)
            S.op("s", s_rstd0, inc=("s", "rstd0"))
            S.wait("v", ("s", "rstd0"))

            def v_t0n(v):
                v.tensor_scalar(smI[0:1, 4:5], smI[0:1, 1:2], -1.0, smI[0:1, 3:4],
                                OP.mult, OP.mult)
                v.drain()
                v.tensor_scalar(t0n[0:1, :], tok0S[:, :], smI[0:1, 3:4],
                                smI[0:1, 4:5], OP.mult, OP.add)
                last = None
                for idx, bs in enumerate([4, 16, 64, 256, 1024]):
                    v.reduce_sum(smI[0:1, 5:6], t0n[0:1, 0:bs], axis=AX)
                    v.drain()
                    v.tensor_scalar(smI[0:1, 6:7], smI[0:1, 5:6], 1.0 / bs, None,
                                    OP.mult)
                    v.drain()
                    v.tensor_scalar(scrR[0:1, 0:bs], t0n[0:1, 0:bs], smI[0:1, 6:7],
                                    None, OP.is_gt, OP.add,
                                    accum_out=smI[0:1, 5:6])
                    v.drain()
                    last = v.tensor_scalar(cnts[0:1, idx:idx + 1], smI[0:1, 5:6],
                                           1.0, None, OP.max)
                return last
            S.op("v", v_t0n, inc=("v", "cnts"))
            S.wait("s", ("v", "cnts"))

            def s_ly(s):
                s.activation(lyS[0:1, :], cnts[0:1, :], AF.Ln)
                return s.activation(smI[0:1, 7:8], smI[0:1, 0:1], AF.Ln)
            S.op("s", s_ly, inc=("s", "ly"))
            S.wait("v", ("s", "ly"))

            def v_D(v):
                v.tensor_tensor(scr5[0:1, :], lyS[0:1, :], cvecS[0:1, :], OP.mult)
                v.tensor_scalar(smI[0:1, 9:10], smI[0:1, 7:8], 50.0, None, OP.mult)
                v.drain()
                v.reduce_sum(smI[0:1, 8:9], scr5[0:1, :], axis=AX)
                v.reciprocal(smI[0:1, 10:11], smI[0:1, 9:10])
                v.drain()
                v.tensor_scalar(smI[0:1, 11:12], smI[0:1, 8:9], smI[0:1, 10:11],
                                -1.0, OP.mult, OP.mult)
                v.drain()
                return v.tensor_scalar(smI[0:1, 12:13], smI[0:1, 11:12], 1.0, 2.0,
                                       OP.max, OP.min)
            S.op("v", v_D, inc=("v", "D"))

            S.wait("t", ("v", "D"))
            S.wait("t", ("g", "consts"))
            S.op("t", lambda t: t.matmul(pH[:, 40:41], onesf[:, :], smI[0:1, 12:13],
                                         start=True, stop=True), inc=("t", "Db"))
            S.wait("s", ("t", "Db"))
            S.op("s", lambda s: s.copy(filt[:, 0:1], pH[:, 40:41]), inc=("s", "Db"))

            S.wait("v", ("s", "Db"))

            def v_filt1(v):
                v.tensor_tensor(filt[:, 1:2], filt[:, 0:1], filt[:, 0:1], OP.mult)
                v.tensor_scalar(filt[:, 3:4], filt[:, 0:1], 0.5, 1.0, OP.mult, OP.add)
                v.tensor_scalar(filt[:, 6:7], filt[:, 0:1], C01_HI, None, OP.mult)
                v.drain()
                v.scalar_tensor_tensor(filt[:, 4:5], filt[:, 0:1], C01_LO,
                                       filt[:, 6:7], OP.mult, OP.add)
                v.tensor_scalar(filt[:, 5:6], filt[:, 1:2], C02_HI, None, OP.mult)
                v.drain()
                v.tensor_tensor(filt[:, 7:8], filt[:, 5:6], filt[:, 0:1], OP.add)
                v.drain()
                v.scalar_tensor_tensor(filt[:, 2:3], filt[:, 1:2], C02_LO,
                                       filt[:, 7:8], OP.mult, OP.add)
                v.drain()
                return v.tensor_scalar(filt[:, 8:16], kaS[:, :], filt[:, 3:4],
                                       filt[:, 2:3], OP.mult, OP.add)
            S.op("v", v_filt1, inc=("v", "u1"))

            def v_piarg(v):
                v.drain()
                v.tensor_scalar(filt[:, 56:64], filt[:, 8:16], -1.0, PI_HI,
                                OP.mult, OP.add)
                v.drain()
                return v.tensor_scalar(filt[:, 56:64] if False else filt[:, 64:72],
                                       filt[:, 56:64], PI_LO, None, OP.add)
            S.op("v", v_piarg, inc=("v", "piarg"))
            S.wait("s", ("v", "piarg"))

            def s_real(s):
                s.activation(filt[:, 32:40], filt[:, 64:72], AF.Sin)
                return s.activation(filt[:, 16:24], kaS[:, :], AF.Square)
            S.op("s", s_real, inc=("s", "real"))
            S.wait("v", ("s", "real"))

            def v_phase(v):
                v.scalar_tensor_tensor(filt[:, 8:16], filt[:, 16:24], filt[:, 4:5],
                                       kaS[:, :], OP.mult, OP.subtract)
                v.drain()
                return v.tensor_scalar(filt[:, 24:32], filt[:, 8:16], filt[:, 2:3],
                                       None, OP.add)
            S.op("v", v_phase, inc=("v", "phase"))

            def v_pi2arg(v):
                v.drain()
                v.tensor_scalar(filt[:, 56:64], filt[:, 24:32], -1.0, PI2_HI,
                                OP.mult, OP.add)
                v.drain()
                return v.tensor_scalar(filt[:, 64:72], filt[:, 56:64], PI2_LO,
                                       None, OP.add)
            S.op("v", v_pi2arg, inc=("v", "pi2arg"))
            S.wait("s", ("v", "pi2arg"))

            def s_ph(s):
                s.activation(filt[:, 40:48], filt[:, 24:32], AF.Sin)
                return s.activation(filt[:, 48:56], filt[:, 64:72], AF.Sin)
            S.op("s", s_ph, inc=("s", "ph"))
            S.wait("v", ("s", "ph"))

            def v_fr(v):
                v.tensor_scalar(filt[:, 64:72], filt[:, 32:40], 1.0 / 1024, None,
                                OP.mult)
                v.drain()
                v.tensor_tensor(filt[:, 56:64], filt[:, 64:72], filt[:, 48:56],
                                OP.mult)
                return v.tensor_tensor(filt[:, 72:80], filt[:, 64:72], filt[:, 40:48],
                                       OP.mult)
            S.op("v", v_fr, inc=("v", "fr"))

            def v_frb(v):
                v.drain()
                v.tensor_tensor(filt[:, 24:32], x2b[0][:, 0:8], filt[:, 56:64],
                                OP.subtract)
                v.tensor_tensor(filt[:, 32:40], x2b[0][:, 8:16], filt[:, 72:80],
                                OP.subtract)
                v.tensor_scalar(filt[:, 5:6], filt[:, 0:1], 1.0, None, OP.is_le)
                v.drain()
                v.tensor_scalar(filt[:, 40:48], filt[:, 24:32], filt[:, 5:6],
                                None, OP.mult)
                v.tensor_scalar(filt[:, 48:56], filt[:, 32:40], filt[:, 5:6],
                                None, OP.mult)
                v.drain()
                v.tensor_tensor(filt[:, 8:16], filt[:, 40:48], filt[:, 56:64],
                                OP.add)
                return v.tensor_tensor(filt[:, 16:24], filt[:, 48:56],
                                       filt[:, 72:80], OP.add)
            S.op("v", v_frb, inc=("v", "frb"))

            for c in range(32):
                if c < 2:
                    S.wait("g", ("v", "cnts"))
                else:
                    S.wait("g", ("t", "base", c - 2))
                src, kc = base_srcs[c]
                S.op("g", lambda g, src=src, kc=kc, c=c: g.dma_start(
                    x3b[c % 2][:, :], src[kc * 128:(kc + 1) * 128, :]).then_inc(
                    sems[f"s_b{c % 2}"], 16))

            for kc in range(8):
                S.op("g", lambda g, kc=kc: g.dma_start(
                    W1bf[:, kc * 4096:(kc + 1) * 4096],
                    dw1[kc * 128:(kc + 1) * 128, :]).then_inc(sems["s_w1"], 16))
            for kc2 in range(32):
                S.op("g", lambda g, kc2=kc2: g.dma_start(
                    W2bf[:, kc2 * 1024:(kc2 + 1) * 1024],
                    dw2[kc2 * 128:(kc2 + 1) * 128, :]).then_inc(sems["s_w2"], 16))

            S.wait("t", ("v", "frb"))
            ncs = {"cos": 0, "sin": 0}
            for c, (src, kc) in enumerate(base_srcs):
                S.wait_dma("t", f"s_b{c % 2}", 16 * (c // 2 + 1))
                is_sin = (c % 4) >= 2
                key = "sin" if is_sin else "cos"
                first = ncs[key] == 0
                lastf = ncs[key] == 15
                ncs[key] += 1
                lhs_col = (16 + kc) if is_sin else (8 + kc)
                dstp = pA1 if is_sin else pA0
                ring = x3b[c % 2]

                def t_base(t, lhs_col=lhs_col, dstp=dstp, first=first, lastf=lastf,
                           ring=ring):
                    t.matmul(dstp[0:1, 0:512], filt[:, lhs_col:lhs_col + 1],
                             ring[:, 0:512], start=first, stop=lastf)
                    return t.matmul(dstp[0:1, 512:1024], filt[:, lhs_col:lhs_col + 1],
                                    ring[:, 512:1024], start=first, stop=lastf)
                S.op("t", t_base, inc=("t", "base", c))

            S.wait("s", ("t", "base", 31))
            S.op("s", lambda s: s.copy(scrR[0:1, 0:1024], pA0[0:1, 0:1024]),
                 inc=("s", "screvac"))
            S.wait("v", ("s", "screvac"))

            def v_hr(v):
                v.tensor_tensor(hr2b[0:1, 0:1024], scrR[0:1, 0:1024], pA1[0:1, 0:1024],
                                OP.subtract)
                v.tensor_copy(hr2b[0:1, 1024:2048], hr2b[0:1, 0:1024])
                v.tensor_tensor(HFs[0][0:1, 2048:3072], scrR[0:1, 0:1024],
                                pA1[0:1, 0:1024], OP.add)
                return v.tensor_copy(HFs[0][0:1, 3072:4096], HFs[0][0:1, 2048:3072])
            S.op("v", v_hr, inc=("v", "hr2b"))

            S.wait("t", ("v", "hr2b"))
            S.wait("t", ("g", "consts"))
            for rnd in range(4):
                srcbuf = HFs[0]
                qofs = rnd * 1024
                if rnd > 0:
                    S.wait("t", ("s", "hrbc", rnd - 1))

                def t_bc(t, srcbuf=srcbuf, qofs=qofs):
                    t.matmul(pA0[:, 0:512], onesb[:, :],
                             srcbuf[0:1, qofs:qofs + 512], start=True, stop=True)
                    return t.matmul(pA0[:, 512:1024], onesb[:, :],
                                    srcbuf[0:1, qofs + 512:qofs + 1024],
                                    start=True, stop=True)
                S.op("t", t_bc, inc=("t", "hrbc", rnd))
                S.wait("s", ("t", "hrbc", rnd))
                S.op("s", lambda s, rnd=rnd: s.copy(
                    HFs[0][:, rnd * 1024:(rnd + 1) * 1024], pA0[:, 0:1024]),
                    inc=("s", "hrbc", rnd))

            S.wait("y", ("s", "hrbc", 3))
            S.op("y", lambda y: y.dma_start(
                TTbuf[:, 0:1024],
                bass.AP(HFs[0], 2048 + 128, [[4095, 128], [128, 8], [1, 128]])
            ).then_inc(sems["s_tt"], 16))
            S.op("y", lambda y: y.dma_start(
                Tring[:, 0:1024],
                bass.AP(HFs[0], 1024, [[4095, 128], [128, 8], [1, 128]])
            ).then_inc(sems["s_tr"], 16))
            S.op("s", lambda s: s.dma_start(
                Tring[:, 1024:1408],
                bass.AP(HFs[0], 1024, [[4095, 128], [128, 3], [1, 128]])
            ).then_inc(sems["s_tr"], 16))

            S.wait_dma("t", "s_tt", 16)
            S.wait_dma("t", "s_sm", N_SM)
            for kc in range(8):
                if kc > 0:
                    S.wait("t", ("s", "cw", kc - 1))

                def t_cw(t, kc=kc):
                    last = None
                    for mc in range(8):
                        dtile = (mc - kc) % 8
                        last = t.matmul(pH[:, 64:88],
                                        TTbuf[:, (7 - dtile) * 128:(8 - dtile) * 128],
                                        WelBF[:, mc * 24:(mc + 1) * 24],
                                        start=(mc == 0), stop=(mc == 7))
                    return last
                S.op("t", t_cw, inc=("t", "cw", kc))
                S.wait("s", ("t", "cw", kc))
                S.op("s", lambda s, kc=kc: s.copy(CWbf[:, kc * 24:(kc + 1) * 24],
                                                  pH[:, 64:88]),
                     inc=("s", "cw", kc))

            def v_ssC(v):
                return v.scalar_tensor_tensor(scrC[0:8, :], centS[0:8, :], 0.0,
                                              centS[0:8, :], OP.bypass, OP.mult,
                                              accum_out=rscp[:, 0:1])
            S.op("v", v_ssC, inc=("v", "ssC"))
            S.wait("s", ("v", "ssC"))

            def s_rsc(s):
                s.activation(rscp[:, 1:2], rscp[:, 0:1], AF.Ln)
                s.drain()
                return s.activation(rscp[:, 3:4], rscp[:, 1:2], AF.Exp, scale=0.5)
            S.op("s", s_rsc, inc=("s", "rscn"))
            S.wait("v", ("s", "rscn"))

            def v_rsc(v):
                v.tensor_scalar(rscp[:, 2:3], rscp[:, 3:4], 1e-12, None, OP.max)
                v.drain()
                return v.reciprocal(rscp[:, 0:1], rscp[:, 2:3])
            S.op("v", v_rsc, inc=("v", "rsc"))
            S.wait("t", ("v", "rsc"))
            S.op("t", lambda t: t.transpose(pH[0:1, 48:56], rscp[0:8, 0:1],
                                            eyef[0:8, 0:8]), inc=("t", "rscT"))
            S.wait("s", ("t", "rscT"))
            S.op("s", lambda s: s.copy(rrow[0:1, :], pH[0:1, 48:56]),
                 inc=("s", "rrow"))
            S.wait("t", ("s", "rrow"))
            S.op("t", lambda t: t.matmul(pH[:, 48:56], onesf[:, :], rrow[0:1, :],
                                         start=True, stop=True), inc=("t", "rscBmm"))
            S.wait("s", ("t", "rscBmm"))
            S.op("s", lambda s: s.copy(rscB[:, :], pH[:, 48:56]), inc=("s", "rscB"))

            def st_x0load(i):
                par = i % 2
                if i >= 2:
                    S.wait("y", ("v", "x2", i - 2))
                    S.op("y", lambda y, i=i, par=par: y.dma_start(
                        x0b[par][:, :], dx[i * 128:(i + 1) * 128, :]).then_inc(
                        sems[f"s_x{par}"], 16))

            def st_ln1(i):
                par = i % 2
                S.wait_dma("v", f"s_x{par}", 16 * (i // 2 + 1))
                if i >= 2:
                    S.wait("v", ("t", "yFtr", i - 2))

                def v_ln1(v, par=par):
                    v.bn_stats(stat12[:, 0:6], x0b[par][:, 0:512])
                    v.bn_stats(stat12[:, 6:12], x0b[par][:, 512:1024])
                    v.drain()
                    return v.bn_aggr(mv[:, 0:2], stat12[:, :])
                S.op("v", v_ln1, inc=("v", "stats1", i))
                S.wait("s", ("v", "stats1", i))

                def s_rstd1(s):
                    s.activation(sm1[:, 10:11], mv[:, 1:2], AF.Ln, bias=kst[:, 0:1])
                    s.drain()
                    return s.activation(sm1[:, 0:1], sm1[:, 10:11], AF.Exp,
                                        scale=-0.5)
                S.op("s", s_rstd1, inc=("s", "rstd1", i))
                S.wait("v", ("s", "rstd1", i))

                def v_u(v, par=par):
                    v.tensor_scalar(sm1[:, 1:2], mv[:, 0:1], -1.0, sm1[:, 0:1],
                                    OP.mult, OP.mult)
                    v.drain()
                    return v.tensor_scalar(bfA[par][:, :], x0b[par][:, :],
                                           sm1[:, 0:1], sm1[:, 1:2], OP.mult, OP.add)
                S.op("v", v_u, inc=("v", "u", i))

            def st_uFtr(i):
                par = i % 2
                S.wait("t", ("v", "u", i))
                pt_batch([(j * 128, bfA[par][:, j * 128:(j + 1) * 128])
                          for j in range(8)],
                         lambda s, slot, par=par: s.copy(bfF[par][:, :],
                                                         pT[:, slot:slot + 1024]),
                         t_alias=("t", "uFtr", i), s_alias=("s", "uFevac", i))

            def st_x1(i):
                par = i % 2
                S.wait("t", ("s", "uFevac", i))
                S.wait_dma("t", "s_tr", 32)
                S.wait_dma("t", "s_tt", 16)
                if i >= 2:
                    S.wait("t", ("v", "x3", i - 2))

                def t_x1(t, par=par):
                    last = None
                    for h in range(2):
                        for kc in range(8):
                            j0 = ((h * 4 - kc) % 8) * 128
                            last = t.matmul(pA[par][:, h * 512:(h + 1) * 512],
                                            bfF[par][:, kc * 128:(kc + 1) * 128],
                                            Tring[:, j0:j0 + 512],
                                            start=(kc == 0), stop=False)
                    return last
                S.op("t", t_x1, inc=("t", "x1", i))

            def st_lat(i):
                par = i % 2
                g_ = i // 2
                lofs = 0 if par == 0 else 512
                S.wait("t", ("s", "cw", 7))
                if i >= 2:
                    S.wait("t", ("v", "bits", i - 2))
                if g_ >= 1:
                    S.wait("t", ("s", "gelu", (2 * (g_ - 1) + 1, 6 + par)))

                def t_lat(t, par=par, lofs=lofs):
                    for kc in range(8):
                        t.matmul(pH[:, lofs:lofs + 24],
                                 bfF[par][:, kc * 128:(kc + 1) * 128],
                                 CWbf[:, kc * 24:(kc + 1) * 24],
                                 start=(kc == 0), stop=False)
                    return t.matmul(pH[:, lofs:lofs + 24], onesb[:, :],
                                    belbf[0:1, :], start=False, stop=True)
                S.op("t", t_lat, inc=("t", "lat", i))

            def st_bits(i):
                par = i % 2
                lofs = 0 if par == 0 else 512
                S.wait("v", ("t", "lat", i))

                def v_bits(v, lofs=lofs):
                    v.tensor_scalar(bitsf[:, :], pH[:, lofs:lofs + 24], 0.0, None,
                                    OP.is_gt)
                    v.drain()
                    return v.tensor_copy(bitsb[:, :], bitsf[:, :])
                S.op("v", v_bits, inc=("v", "bits", i))

            def st_bitsF(i):
                S.wait("t", ("v", "bits", i))
                pt_batch([(0, bitsb[:, 0:24])],
                         lambda s, slot: s.copy(tFbf[:, :],
                                                pT[0:24, slot:slot + 128]),
                         t_alias=("t", "bitsFtr", i), s_alias=("s", "bitsFev", i))

            def st_syn(i):
                par = i % 2
                sofs = 128 if par == 0 else 640
                S.wait("t", ("s", "bitsFev", i))
                if i >= 2:
                    S.wait("t", ("v", "corr", i - 2))

                def t_syn(t, sofs=sofs):
                    return t.matmul(pH[:, sofs:sofs + 12], tFbf[0:24, :],
                                    Htbf[0:24, :], start=True, stop=True)
                S.op("t", t_syn, inc=("t", "syn", i))

            def st_corr(i):
                par = i % 2
                sofs = 128 if par == 0 else 640
                S.wait("v", ("t", "syn", i))

                def v_corr(v, sofs=sofs):
                    v.tensor_scalar(syn01[:, :], pH[:, sofs:sofs + 12], 0.5, None,
                                    OP.is_gt)
                    v.tensor_scalar(gsc[:, 0:12], bitsf[:, 0:12], -2.0, 1.0,
                                    OP.mult, OP.add)
                    v.drain()
                    v.reduce_sum(gsc[:, 12:13], syn01[:, :], axis=AX)
                    v.drain()
                    v.tensor_scalar(gsc[:, 13:14], gsc[:, 12:13], 0.5, None,
                                    OP.is_ge)
                    v.tensor_scalar(gsc[:, 14:15], gsc[:, 12:13], 3.5, None,
                                    OP.is_le)
                    v.drain()
                    v.tensor_tensor(gsc[:, 15:16], gsc[:, 13:14], gsc[:, 14:15],
                                    OP.mult)
                    v.drain()
                    v.tensor_scalar(corrf[:, 12:24], syn01[:, :], gsc[:, 15:16],
                                    None, OP.mult)
                    v.drain()
                    v.scalar_tensor_tensor(corrf[:, 0:12], gsc[:, 0:12], 0.0,
                                           corrf[:, 12:24], OP.bypass, OP.mult)
                    v.drain()
                    v.tensor_tensor(gsc[:, 0:12], corrf[:, 0:12], bitsf[:, 0:12],
                                    OP.add)
                    v.drain()
                    v.tensor_copy(corrf[:, 0:12], gsc[:, 0:12])
                    v.tensor_copy(corrf[:, 12:24], bitsf[:, 12:24])
                    v.drain()
                    return v.tensor_copy(corrb[:, :], corrf[:, :])
                S.op("v", v_corr, inc=("v", "corr", i))

            def st_wle(i):
                par = i % 2
                S.wait("t", ("v", "corr", i))
                pt_batch([(0, corrb[:, 0:24])],
                         lambda s, slot: s.copy(tFbf[:, :],
                                                pT[0:24, slot:slot + 128]),
                         t_alias=("t", "corrFtr", i), s_alias=("s", "corrFev", i))
                S.wait("t", ("s", "corrFev", i))

                def t_wle(t, par=par):
                    if _STAGE != "nowle":
                        t.matmul(pA[par][:, 0:512], tFbf[0:24, :], WleBF[:, 0:512],
                                 start=False, stop=False)
                        t.matmul(pA[par][:, 512:1024], tFbf[0:24, :],
                                 WleBF[:, 512:1024], start=False, stop=False)
                    t.matmul(pA[par][:, 0:512], onesb[:, :], blebf[0:1, 0:512],
                             start=False, stop=True)
                    return t.matmul(pA[par][:, 512:1024], onesb[:, :],
                                    blebf[0:1, 512:1024], start=False, stop=True)
                S.op("t", t_wle, inc=("t", "attn", i))

            def st_x2ln2(i):
                par = i % 2
                S.wait("v", ("t", "attn", i))
                S.op("v", lambda v, par=par: v.scalar_tensor_tensor(
                    x2b[par][:, :], x0b[par][:, :], 0.0, pA[par][:, :],
                    OP.bypass, OP.add),
                    inc=("v", "x2", i))

                def v_ln2(v, par=par):
                    v.bn_stats(stat12[:, 0:6], x2b[par][:, 0:512])
                    v.bn_stats(stat12[:, 6:12], x2b[par][:, 512:1024])
                    v.drain()
                    return v.bn_aggr(sm1[:, 6:8], stat12[:, :])
                S.op("v", v_ln2, inc=("v", "stats2", i))
                S.wait("s", ("v", "stats2", i))

                def s_rstd2(s):
                    s.activation(sm1[:, 10:11], sm1[:, 7:8], AF.Ln, bias=kst[:, 0:1])
                    s.activation(sm1[:, 11:12], sm1[:, 7:8], AF.Ln, scale=1024.0)
                    s.drain()
                    s.activation(sm1[:, 5:6], sm1[:, 10:11], AF.Exp, scale=-0.5)
                    return s.activation(sm1[:, 8:9], sm1[:, 11:12], AF.Exp,
                                        scale=0.5)
                S.op("s", s_rstd2, inc=("s", "rstd2", i))
                S.wait("v", ("s", "rstd2", i))
                S.wait("v", ("t", "uFtr", i))

                def v_y(v, par=par):
                    v.tensor_scalar(sm1[:, 2:3], sm1[:, 6:7], -1.0, sm1[:, 5:6],
                                    OP.mult, OP.mult)
                    v.tensor_tensor(sm1[:, 9:10], sm1[:, 8:9], sm1[:, 5:6],
                                    OP.mult)
                    v.drain()
                    return v.tensor_scalar(bfA[par][:, :], x2b[par][:, :],
                                           sm1[:, 5:6], sm1[:, 2:3], OP.mult, OP.add)
                S.op("v", v_y, inc=("v", "y", i))

                def v_rn(v, par=par):
                    v.tensor_scalar(sm1[:, 3:4], sm1[:, 9:10], 1e-12, None, OP.max)
                    v.drain()
                    return v.reciprocal(sm1[:, 12 + par:13 + par], sm1[:, 3:4])
                S.op("v", v_rn, inc=("v", "rn", i))

            def st_yFtr(i):
                par = i % 2
                S.wait("t", ("v", "y", i))
                pt_batch([(j * 128, bfA[par][:, j * 128:(j + 1) * 128])
                          for j in range(8)],
                         lambda s, slot, par=par: s.copy(bfF[par][:, :],
                                                         pT[:, slot:slot + 1024]),
                         t_alias=("t", "yFtr", i), s_alias=("s", "yFevac", i))

            def st_gating(i):
                par = i % 2
                gofs = 240 if par == 0 else 752
                S.wait("t", ("s", "yFevac", i))
                S.wait("t", ("v", "x2", i + 1 if par == 0 else i))
                if i >= 2:
                    S.wait("t", ("v", "gbf", i - 2))
                    S.wait("t", ("v", "x3", i - 1 if par == 0 else i - 2))
                else:
                    S.wait("t", ("s", "hrbc", 3))

                def t_gat(t, par=par, gofs=gofs):
                    last = None
                    for kc in range(8):
                        last = t.matmul(pA1[:, gofs:gofs + 16],
                                        bfF[par][:, kc * 128:(kc + 1) * 128],
                                        CWr[:, kc * 16:(kc + 1) * 16],
                                        start=(kc == 0), stop=(kc == 7))
                    return last
                S.op("t", t_gat, inc=("t", "gat", i))

            def st_gd8(i):
                par = i % 2
                if i >= 1:
                    S.wait("v", ("t", "hftr", (i - 1, 7)))

                def v_gd8(v, par=par):
                    last = None
                    for oc in range(8):
                        last = v.tensor_scalar(
                            gdiag[:, oc * 128:(oc + 1) * 128], eyeb[:, :],
                            g8[par][:, oc:oc + 1], None, OP.mult)
                    return last
                S.op("v", v_gd8, inc=("v", "gd8", i))

            def st_gates(i):
                par = i % 2
                gofs = 240 if par == 0 else 752
                S.wait("v", ("t", "gat", i))
                S.wait("v", ("s", "rscB"))
                S.wait("v", ("v", "rn", i)) if False else None

                def v_gate1(v, gofs=gofs, par=par):
                    v.tensor_tensor(lgt[:, 32:40], pA1[:, gofs:gofs + 8],
                                    rscB[:, :], OP.mult)
                    v.drain()
                    v.scalar_tensor_tensor(lgt[:, 0:8], lgt[:, 32:40],
                                           sm1[:, 12 + par:13 + par],
                                           pA1[:, gofs + 8:gofs + 16], OP.mult,
                                           OP.add)
                    v.drain()
                    v.reduce_max(lgt[:, 40:41], lgt[:, 0:8], axis=AX)
                    v.reduce_max(lgt[:, 41:42], lgt[:, 0:8], axis=AX, negate=True)
                    v.drain()
                    v.tensor_scalar(lgt[:, 8:16], lgt[:, 0:8], lgt[:, 40:41], None,
                                    OP.is_ge, OP.add, accum_out=lgt[:, 42:43])
                    v.drain()
                    v.scalar_tensor_tensor(lgt[:, 16:24], lgt[:, 8:16], -1e30,
                                           lgt[:, 0:8], OP.mult, OP.add)
                    v.tensor_scalar(lgt[:, 44:45], lgt[:, 42:43], 1.5, None,
                                    OP.is_ge)
                    v.drain()
                    v.reduce_max(lgt[:, 43:44], lgt[:, 16:24], axis=AX)
                    v.drain()
                    v.tensor_tensor(lgt[:, 45:46], lgt[:, 40:41], lgt[:, 43:44],
                                    OP.subtract)
                    v.drain()
                    v.tensor_tensor(lgt[:, 46:47], lgt[:, 44:45], lgt[:, 45:46],
                                    OP.mult)
                    v.drain()
                    v.tensor_tensor(lgt[:, 45:46], lgt[:, 46:47], lgt[:, 43:44],
                                    OP.add)
                    v.drain()
                    v.tensor_scalar(lgt[:, 48:49], lgt[:, 41:42], 0.5, None,
                                    OP.mult)
                    return v.tensor_scalar(lgt[:, 8:16], lgt[:, 0:8],
                                           lgt[:, 45:46], None, OP.is_ge)
                S.op("v", v_gate1, inc=("v", "mask", i))
                S.wait("s", ("v", "mask", i))
                S.op("s", lambda s: s.activation(lgt[:, 24:32], lgt[:, 0:8],
                                                 AF.Tanh, scale=0.5,
                                                 bias=lgt[:, 48:49]),
                     inc=("s", "E", i))
                S.wait("v", ("s", "E", i))

                def v_gate2(v, par=par):
                    v.tensor_scalar(lgt[:, 16:24], lgt[:, 24:32], 1.0, None,
                                    OP.add)
                    v.tensor_scalar(lgt[:, 0:8], lgt[:, 24:32], -1.0, 1.0,
                                    OP.mult, OP.add)
                    v.drain()
                    v.reciprocal(lgt[:, 24:32], lgt[:, 0:8])
                    v.drain()
                    v.tensor_tensor(lgt[:, 0:8], lgt[:, 16:24], lgt[:, 24:32],
                                    OP.mult)
                    v.drain()
                    v.scalar_tensor_tensor(lgt[:, 32:40], lgt[:, 0:8], 0.0,
                                           lgt[:, 8:16], OP.bypass, OP.mult,
                                           accum_out=lgt[:, 46:47])
                    v.drain()
                    v.reciprocal(lgt[:, 47:48], lgt[:, 46:47])
                    v.drain()
                    v.tensor_scalar(g8[par][:, :], lgt[:, 32:40], lgt[:, 47:48],
                                    None, OP.mult)
                    v.drain()
                    return v.tensor_copy(g8b[par][:, :], g8[par][:, :])
                S.op("v", v_gate2, inc=("v", "gbf", i))

                if par == 0:
                    st_gd8(i)

            def st_gFtr(i):
                par = i % 2
                S.wait("t", ("v", "gbf", i))
                pt_batch([(0, g8b[par][:, 0:8])],
                         lambda s, slot, par=par: s.copy(gFbf[par][:, :],
                                                         pT[0:8, slot:slot + 128]),
                         t_alias=("t", "gFtr", i), s_alias=("s", "gFev", i))

            def ffn_ops(g_, ops):
                i0, i1 = 2 * g_, 2 * g_ + 1
                chunks = [(i0, oc) for oc in range(8)] + \
                         [(i1, oc) for oc in range(8)]
                hps = [pA[0][:, 0:512], pA[0][:, 512:1024],
                       pH[:, 0:512], pH[:, 512:1024]]
                for kind, idx_c in ops:
                    if kind == "gd8":
                        st_gd8(i1)
                        continue
                    i2, oc = chunks[idx_c]
                    p2 = i2 % 2
                    buf = idx_c % 4
                    if kind == "H":
                        S.wait("t", ("s", "yFevac", i2))
                        if idx_c >= 4:
                            S.wait("t", ("s", "gelu", chunks[idx_c - 4]))
                        if idx_c == 0:
                            S.wait_dma("t", "s_w1", 16 * 8)
                            S.wait("t", ("v", "x2", i0))
                        if g_ == 0 and idx_c == 2:
                            S.wait("t", ("s", "rscB"))

                        def t_H(t, p2=p2, oc=oc, buf=buf):
                            for kc in range(8):
                                t.matmul(hps[buf],
                                         bfF[p2][:, kc * 128:(kc + 1) * 128],
                                         W1bf[:, kc * 4096 + oc * 512:
                                              kc * 4096 + (oc + 1) * 512],
                                         start=(kc == 0), stop=False)
                            return t.matmul(hps[buf], onesb[:, :],
                                            b1bf[0:1, oc * 512:(oc + 1) * 512],
                                            start=False, stop=True)
                        S.op("t", t_H, inc=("t", "H", (i2, oc)))

                        S.wait("s", ("t", "H", (i2, oc)))
                        if idx_c >= 4:
                            S.wait("s", ("t", "hftr", chunks[idx_c - 4]))
                        if g_ == 0 and idx_c == 0:
                            S.wait_dma("s", "s_tr", 32)

                        def s_gelu(s, buf=buf):
                            return s.activation(Htmp[buf][:, :], hps[buf], AF.Gelu)
                        S.op("s", s_gelu, inc=("s", "gelu", (i2, oc)))

                    else:
                        S.wait("t", ("s", "gelu", (i2, oc)))
                        S.wait("t", ("v", "gd8", i2))
                        if (ptb["n"] % 2) == 1 and g_ not in s1guard:
                            s1guard.add(g_)
                            S.wait("t", ("v", "mask", 2 * g_ + 1))
                        pt_batch([(j * 128,
                                   Htmp[buf][:, j * 128:(j + 1) * 128])
                                  for j in range(4)],
                                 lambda e, slot, i2=i2, oc=oc: e.tensor_copy(
                                     HFs[i2 % 2][:, oc * 512:(oc + 1) * 512],
                                     pA1[:, slot // 2:slot // 2 + 512]),
                                 t_alias=("t", "hftr", (i2, oc)),
                                 s_alias=("s", "hfev", (i2, oc)),
                                 rhs=gdiag[:, oc * 128:(oc + 1) * 128],
                                 evac_eng="v")

            def st_w2_one(g_, i2):
                i0, i1 = 2 * g_, 2 * g_ + 1
                chunks = [(i0, oc) for oc in range(8)] + \
                         [(i1, oc) for oc in range(8)]
                if True:
                    p2 = i2 % 2
                    for oc in range(8):
                        S.wait("t", ("s", "hfev", (i2, oc)))
                    S.wait("t", ("v", "x2", i2))
                    S.wait("t", ("s", "gFev", i2))
                    if i2 == i0:
                        S.wait("t", ("s", "gelu", chunks[12]))
                        S.wait("t", ("s", "gelu", chunks[13]))
                    S.wait_dma("t", "s_w2", 16 * 32)

                    def t_w2(t, p2=p2):
                        last = None
                        for dc in range(2):
                            for kc2 in range(32):
                                last = t.matmul(
                                    pA[p2][:, dc * 512:(dc + 1) * 512],
                                    HFs[p2][:, kc2 * 128:(kc2 + 1) * 128],
                                    W2bf[:, kc2 * 1024 + dc * 512:
                                         kc2 * 1024 + (dc + 1) * 512],
                                    start=(kc2 == 0), stop=False)
                            last = t.matmul(pA[p2][:, dc * 512:(dc + 1) * 512],
                                            gFbf[p2][0:8, :],
                                            b2bf[0:8, dc * 512:(dc + 1) * 512],
                                            start=False, stop=True)
                        return last
                    S.op("t", t_w2, inc=("t", "W2", i2))

                    S.wait("v", ("t", "W2", i2))
                    if i2 >= 2:
                        S.wait_dma("v", f"s_st{i2 % 2}", 16 * (i2 // 2))

                    def v_x3(v, p2=p2):
                        return v.tensor_tensor(x3b[p2][:, :], x2b[p2][:, :],
                                               pA[p2][:, :], OP.add)
                    S.op("v", v_x3, inc=("v", "x3", i2))

                    S.wait("s", ("v", "x3", i2))
                    srcb = x3b if _STAGE == "full" else x2b
                    S.op("s", lambda s, i2=i2, srcb=srcb: s.dma_start(
                        do[i2 * 128:(i2 + 1) * 128, :],
                        srcb[i2 % 2][:, :]).then_inc(sems[f"s_st{i2 % 2}"], 16))

            def st_w2_part(g_, i2, part):
                i0 = 2 * g_
                chunks = [(i0, oc) for oc in range(8)] + \
                         [(i0 + 1, oc) for oc in range(8)]
                p2 = i2 % 2
                dc = part // 2
                k0 = (part % 2) * 16
                if part == 0:
                    for oc in range(8):
                        S.wait("t", ("s", "hfev", (i2, oc)))
                    S.wait("t", ("v", "x2", i2))
                    S.wait("t", ("s", "gFev", i2))
                    S.wait_dma("t", "s_w2", 16 * 32)

                def t_w2p(t, p2=p2, dc=dc, k0=k0, part=part):
                    last = None
                    for kc2 in range(k0, k0 + 16):
                        last = t.matmul(
                            pA[p2][:, dc * 512:(dc + 1) * 512],
                            HFs[p2][:, kc2 * 128:(kc2 + 1) * 128],
                            W2bf[:, kc2 * 1024 + dc * 512:
                                 kc2 * 1024 + (dc + 1) * 512],
                            start=(k0 == 0 and kc2 == k0), stop=False)
                    if part % 2 == 1:
                        last = t.matmul(pA[p2][:, dc * 512:(dc + 1) * 512],
                                        gFbf[p2][0:8, :],
                                        b2bf[0:8, dc * 512:(dc + 1) * 512],
                                        start=False, stop=True)
                    return last
                S.op("t", t_w2p, inc=("t", "W2p", (i2, part)))

            def st_w2_fin(g_, i2):
                p2 = i2 % 2
                S.wait("v", ("t", "W2p", (i2, 3)))
                if i2 >= 2:
                    S.wait_dma("v", f"s_st{i2 % 2}", 16 * (i2 // 2))

                def v_x3(v, p2=p2):
                    return v.tensor_tensor(x3b[p2][:, :], x2b[p2][:, :],
                                           pA[p2][:, :], OP.add)
                S.op("v", v_x3, inc=("v", "x3", i2))

                S.wait("s", ("v", "x3", i2))
                srcb = x3b if _STAGE == "full" else x2b
                S.op("s", lambda s, i2=i2, srcb=srcb: s.dma_start(
                    do[i2 * 128:(i2 + 1) * 128, :],
                    srcb[i2 % 2][:, :]).then_inc(sems[f"s_st{i2 % 2}"], 16))

            n_grp = n_sub // 2
            st_ln1(0)
            if n_sub > 1:
                st_ln1(1)
            for g_ in range(n_grp):
                i0, i1 = 2 * g_, 2 * g_ + 1
                st_uFtr(i0)
                st_x1(i0)
                st_uFtr(i1)
                st_lat(i0)
                st_bits(i0)
                st_x1(i1)
                st_bitsF(i0)
                st_lat(i1)
                st_syn(i0)
                st_corr(i0)
                st_bits(i1)
                st_wle(i0)
                st_bitsF(i1)
                st_x2ln2(i0)
                st_syn(i1)
                st_corr(i1)
                st_yFtr(i0)
                st_wle(i1)
                st_x2ln2(i1)
                st_gating(i0)
                ffn_ops(g_, [("H", 0), ("H", 1)])
                st_gates(i0)
                ffn_ops(g_, [("H", 2), ("H", 3)])
                st_yFtr(i1)
                st_gating(i1)
                st_gates(i1)
                ffn_ops(g_, [("H", 4), ("hftr", 0), ("H", 5), ("hftr", 1)])
                st_gFtr(i0)
                st_gFtr(i1)
                seq = [("H", 6), ("hftr", 2), ("H", 7), ("hftr", 3)]
                for idx in range(8, 16):
                    seq += [("H", idx), ("hftr", idx - 4)]
                    if idx == 11:
                        seq += [("gd8", 0)]
                seq += [("hftr", 12), ("hftr", 13), ("hftr", 14), ("hftr", 15)]
                ffn_ops(g_, seq)
                st_w2_one(g_, i0)
                if g_ + 1 < n_grp:
                    st_x0load(i0 + 2)
                    st_x0load(i1 + 2)
                    st_ln1(i0 + 2)
                    st_ln1(i1 + 2)
                st_w2_one(g_, i1)

            n_st0 = (n_sub + 1) // 2
            n_st1 = n_sub // 2
            S.wait_dma("s", "s_st0", 16 * n_st0)
            S.wait_dma("s", "s_st1", 16 * n_st1)

        S.emit_eng = None
        walk()

        @block.tensor
        def _(t):
            S.begin("t", t)
            walk()

        @block.vector
        def _(v):
            S.begin("v", v)
            walk()

        @block.scalar
        def _(s):
            S.begin("s", s)
            walk()

        @block.gpsimd
        def _(g):
            S.begin("g", g)
            walk()

        @block.sync
        def _(y):
            S.begin("y", y)
            walk()

    return nc


def _shard_inputs(L0_attn_x, L0_attn_W_el, L0_attn_b_el, L0_attn_W_le,
                  L0_attn_b_le, L0_ffn_w1, L0_ffn_b1, L0_ffn_w2, L0_ffn_b2,
                  L0_ffn_centroids, L0_ffn_w_route, n_sub=N_SUB_FULL):
    C = _consts()
    f32 = np.float32
    x = np.ascontiguousarray(np.asarray(L0_attn_x, dtype=f32).reshape(-1, 1024))
    tok0 = np.ascontiguousarray(x[0:1, :])
    wel = np.ascontiguousarray(np.asarray(L0_attn_W_el, dtype=f32))
    bel = np.ascontiguousarray(np.asarray(L0_attn_b_el, dtype=f32).reshape(1, 24))
    wle = np.ascontiguousarray(np.asarray(L0_attn_W_le, dtype=f32))
    ble = np.ascontiguousarray(np.asarray(L0_attn_b_le, dtype=f32).reshape(1, 1024))
    w1 = np.asarray(L0_ffn_w1, dtype=f32)
    w1r = np.ascontiguousarray(w1.transpose(1, 0, 2).reshape(1024, 4096))
    b1r = np.ascontiguousarray(np.asarray(L0_ffn_b1, dtype=f32).reshape(1, 4096))
    w2r = np.ascontiguousarray(np.asarray(L0_ffn_w2, dtype=f32).reshape(4096, 1024))
    b2r = np.ascontiguousarray(np.asarray(L0_ffn_b2, dtype=f32))
    cent = np.ascontiguousarray(np.asarray(L0_ffn_centroids, dtype=f32))
    wrt = np.asarray(L0_ffn_w_route, dtype=f32)
    cwr = np.ascontiguousarray(
        np.concatenate([cent.T, wrt.T], axis=1))
    shared = dict(tok0=tok0, wel=wel, bel=bel, wle=wle, ble=ble, w1r=w1r,
                  b1r=b1r, w2r=w2r, b2r=b2r, cwr=cwr, cent=cent,
                  kaF=C["kaF"], filtc=C["filtc"], Ht=C["Ht"], eye=C["eye"],
                  cvec=C["cvec"],
                  coshi=C["coshi"], coslo=C["coslo"], sinhi=C["sinhi"],
                  sinlo=C["sinlo"])
    rows = n_sub * 128
    in_maps = []
    for c in range(N_CORES):
        m = dict(shared)
        m["x"] = x[c * rows:(c + 1) * rows, :]
        in_maps.append(m)
    return in_maps


def kernel(L0_attn_x, L0_attn_W_el, L0_attn_b_el, L0_attn_W_le, L0_attn_b_le,
           L0_ffn_w1, L0_ffn_b1, L0_ffn_w2, L0_ffn_b2, L0_ffn_centroids,
           L0_ffn_w_route):
    in_maps = _shard_inputs(L0_attn_x, L0_attn_W_el, L0_attn_b_el, L0_attn_W_le,
                            L0_attn_b_le, L0_ffn_w1, L0_ffn_b1, L0_ffn_w2,
                            L0_ffn_b2, L0_ffn_centroids, L0_ffn_w_route)
    nc = _build(N_SUB_FULL)
    res = run_bass_kernel_spmd(nc, in_maps, core_ids=list(range(N_CORES)))
    out = np.concatenate([np.asarray(r["o"]) for r in res.results], axis=0)
    return out.reshape(16, 2048, 1024)


# version 09 (final)
# speedup vs baseline: 1.0421x; 1.0421x over previous
import numpy as np
from contextlib import ExitStack

import concourse.bass as bass
import concourse.mybir as mybir
from concourse.bass_utils import run_bass_kernel_spmd
from concourse.alu_op_type import AluOpType as OP

F32 = mybir.dt.float32
BF16 = mybir.dt.bfloat16
AF = mybir.ActivationFunctionType
AX = mybir.AxisListType.X
PI = float(np.pi)
PI_HI = float(np.float32(np.pi))
PI_LO = float(np.float32(np.pi - np.float64(np.float32(np.pi))))
PI2_HI = float(np.float32(np.pi / 2))
PI2_LO = float(np.float32(np.pi / 2 - np.float64(np.float32(np.pi / 2))))
C02_HI = float(np.float32(0.2))
C02_LO = float(np.float32(0.2 - np.float64(np.float32(0.2))))
C01_HI = float(np.float32(0.1))
C01_LO = float(np.float32(0.1 - np.float64(np.float32(0.1))))

N_CORES = 8
N_SUB_FULL = 32
_STAGE = "full"

_A_LIT = np.array(
    [[1, 1, 1, 0, 1, 1, 1, 0, 0, 0, 1, 1], [1, 1, 0, 1, 1, 1, 0, 0, 0, 1, 1, 0],
     [1, 0, 1, 1, 1, 0, 0, 0, 1, 1, 0, 1], [0, 1, 1, 1, 0, 0, 0, 1, 1, 0, 1, 1],
     [0, 0, 0, 0, 1, 1, 1, 1, 1, 1, 1, 1], [1, 1, 1, 1, 0, 0, 0, 0, 0, 0, 0, 0],
     [0, 0, 0, 0, 1, 1, 1, 1, 0, 0, 0, 0], [1, 1, 0, 0, 1, 1, 0, 0, 1, 1, 0, 0],
     [1, 0, 1, 0, 1, 0, 1, 0, 1, 0, 1, 0], [0, 1, 0, 1, 0, 1, 0, 1, 0, 1, 0, 1],
     [1, 0, 0, 1, 1, 0, 0, 1, 1, 0, 0, 1], [0, 1, 1, 0, 0, 1, 1, 0, 0, 1, 1, 0]],
    dtype=np.float32)

_CONSTS = None


def _consts():
    global _CONSTS
    if _CONSTS is not None:
        return _CONSTS
    n = 1024
    k = np.fft.fftfreq(n)
    ka = np.abs(k) + 1e-8
    kaF = np.ascontiguousarray(
        ka.reshape(8, 128).T).astype(np.float32)
    d = np.arange(n)
    ang = 2.0 * np.pi * np.outer(d, d) / n
    cosb = np.cos(ang)
    sinb = np.sin(ang)
    coshi = cosb.astype(np.float32)
    coslo = (cosb - coshi).astype(np.float32)
    sinhi = sinb.astype(np.float32)
    sinlo = (sinb - sinhi).astype(np.float32)
    H = np.concatenate([_A_LIT.T, np.eye(12, dtype=np.float32)], axis=1)
    Ht = np.ascontiguousarray(H.T)
    eye = np.eye(128, dtype=np.float32)
    cvec = np.array([[-10.0, -5.0, 0.0, 5.0, 10.0]], dtype=np.float32)
    try:
        import ctypes
        _lm = ctypes.CDLL("libm.so.6")
        _lm.sinf.restype = ctypes.c_float
        _lm.sinf.argtypes = [ctypes.c_float]
        _lm.cosf.restype = ctypes.c_float
        _lm.cosf.argtypes = [ctypes.c_float]
        _sinf = lambda a: np.array([_lm.sinf(float(t)) for t in a], np.float32)
        _cosf = lambda a: np.array([_lm.cosf(float(t)) for t in a], np.float32)
    except (OSError, AttributeError):
        _sinf = lambda a: np.sin(a.astype(np.float64)).astype(np.float32)
        _cosf = lambda a: np.cos(a.astype(np.float64)).astype(np.float32)
    f32 = np.float32
    ka32 = (np.abs(np.fft.fftfreq(n).astype(f32)) + f32(1e-8)).astype(f32)
    D1 = f32(1.0)
    om1 = (f32(1.0) + f32(0.2) * D1).astype(f32)
    a1 = (f32(1.0) + f32(0.5) * D1).astype(f32)
    b1 = (f32(0.1) * D1).astype(f32)
    arg1 = (om1 * D1 + a1 * ka32).astype(f32)
    ph1 = (om1 * D1 - ka32 + b1 * (ka32 * ka32)).astype(f32)
    re1 = _sinf(arg1)
    frc = ((re1 * _cosf(ph1)).astype(f32).astype(np.float64) / n).astype(f32)
    fic = ((re1 * _sinf(ph1)).astype(f32).astype(np.float64) / n).astype(f32)
    filtc = np.ascontiguousarray(np.concatenate(
        [frc.reshape(8, 128).T, fic.reshape(8, 128).T], axis=1))
    _CONSTS = dict(kaF=kaF, coshi=coshi, coslo=coslo, sinhi=sinhi, sinlo=sinlo,
                   Ht=Ht, eye=eye, cvec=cvec, filtc=filtc)
    return _CONSTS


class _Sched:

    def __init__(self, sems):
        self.ev = {}
        self.cnt = {}
        self.emit_eng = None
        self.eng = None
        self.sems = sems
        self.waited = {}

    def begin(self, emit_eng, eng):
        self.emit_eng = emit_eng
        self.eng = eng
        self.waited = {}

    def op(self, etag, fn, inc=None):
        if self.emit_eng is None:
            if inc is not None:
                sem_name = "s_" + etag
                c = self.cnt.get(sem_name, 0) + 1
                self.cnt[sem_name] = c
                self.ev[inc] = (sem_name, c)
            return
        if etag != self.emit_eng:
            return
        instr = fn(self.eng)
        if inc is not None:
            sem_name, _ = self.ev[inc]
            instr.then_inc(self.sems[sem_name], 1)

    def alias(self, key, src_key):
        if self.emit_eng is None:
            self.ev[key] = self.ev[src_key]

    def wait(self, etag, key):
        if self.emit_eng is None or etag != self.emit_eng:
            return
        sem_name, cntv = self.ev[key]
        if self.waited.get(sem_name, -1) >= cntv:
            return
        self.eng.wait_ge(self.sems[sem_name], cntv)
        self.waited[sem_name] = cntv

    def wait_dma(self, etag, sem_name, value):
        if self.emit_eng is None or etag != self.emit_eng:
            return
        if self.waited.get(sem_name, -1) >= value:
            return
        self.eng.wait_ge(self.sems[sem_name], value)
        self.waited[sem_name] = value


def _build(n_sub=N_SUB_FULL):
    nc = bass.Bass("TRN2", target_bir_lowering=False,
                   detect_race_conditions=False)
    n_tok = n_sub * 128

    dx = nc.dram_tensor("x", [n_tok, 1024], F32, kind="ExternalInput").ap()
    dtok0 = nc.dram_tensor("tok0", [1, 1024], F32, kind="ExternalInput").ap()
    dwel = nc.dram_tensor("wel", [1024, 24], F32, kind="ExternalInput").ap()
    dbel = nc.dram_tensor("bel", [1, 24], F32, kind="ExternalInput").ap()
    dwle = nc.dram_tensor("wle", [25, 1024], F32, kind="ExternalInput").ap()
    dble = nc.dram_tensor("ble", [1, 1024], F32, kind="ExternalInput").ap()
    dw1 = nc.dram_tensor("w1r", [1024, 4096], F32, kind="ExternalInput").ap()
    db1 = nc.dram_tensor("b1r", [1, 4096], F32, kind="ExternalInput").ap()
    dw2 = nc.dram_tensor("w2r", [4096, 1024], F32, kind="ExternalInput").ap()
    db2 = nc.dram_tensor("b2r", [8, 1024], F32, kind="ExternalInput").ap()
    dcwr = nc.dram_tensor("cwr", [1024, 16], F32, kind="ExternalInput").ap()
    dcent = nc.dram_tensor("cent", [8, 1024], F32, kind="ExternalInput").ap()
    dka = nc.dram_tensor("kaF", [128, 8], F32, kind="ExternalInput").ap()
    dfiltc = nc.dram_tensor("filtc", [128, 16], F32, kind="ExternalInput").ap()
    dht = nc.dram_tensor("Ht", [24, 12], F32, kind="ExternalInput").ap()
    deye = nc.dram_tensor("eye", [128, 128], F32, kind="ExternalInput").ap()
    dcvec = nc.dram_tensor("cvec", [1, 5], F32, kind="ExternalInput").ap()
    dcoshi = nc.dram_tensor("coshi", [1024, 1024], F32, kind="ExternalInput").ap()
    dcoslo = nc.dram_tensor("coslo", [1024, 1024], F32, kind="ExternalInput").ap()
    dsinhi = nc.dram_tensor("sinhi", [1024, 1024], F32, kind="ExternalInput").ap()
    dsinlo = nc.dram_tensor("sinlo", [1024, 1024], F32, kind="ExternalInput").ap()
    do = nc.dram_tensor("o", [n_tok, 1024], F32, kind="ExternalOutput").ap()

    with ExitStack() as ctx:
        e = ctx.enter_context
        W1bf = e(nc.sbuf_tensor("W1bf", [128, 32768], BF16))
        W2bf = e(nc.sbuf_tensor("W2bf", [128, 32768], BF16))
        Tring = e(nc.sbuf_tensor("Tring", [128, 1408], BF16))
        HFs = [e(nc.sbuf_tensor(f"HFs{j}", [128, 4096], BF16)) for j in range(2)]
        bfA = [e(nc.sbuf_tensor(f"bfA{j}", [128, 1024], BF16)) for j in range(2)]
        bfF = [e(nc.sbuf_tensor(f"bfF{j}", [128, 1024], BF16)) for j in range(2)]
        x0b = [e(nc.sbuf_tensor(f"x0b{j}", [128, 1024], F32)) for j in range(2)]
        x2b = [e(nc.sbuf_tensor(f"x2b{j}", [128, 1024], F32)) for j in range(2)]
        x3b = [e(nc.sbuf_tensor(f"x3b{j}", [128, 1024], F32)) for j in range(2)]
        Htmp = [e(nc.sbuf_tensor(f"Htmp{j}", [128, 512], BF16)) for j in range(4)]
        gdiag = e(nc.sbuf_tensor("gdiag", [128, 1024], BF16))
        CWbf = e(nc.sbuf_tensor("CWbf", [128, 192], BF16))
        CWr = e(nc.sbuf_tensor("CWr", [128, 128], BF16))
        WelBF = e(nc.sbuf_tensor("WelBF", [128, 192], BF16))
        WleBF = e(nc.sbuf_tensor("WleBF", [25, 1024], BF16))
        Htbf = e(nc.sbuf_tensor("Htbf", [24, 12], BF16))
        b1bf = e(nc.sbuf_tensor("b1bf", [1, 4096], BF16))
        b2bf = e(nc.sbuf_tensor("b2bf", [8, 1024], BF16))
        belbf = e(nc.sbuf_tensor("belbf", [1, 24], BF16))
        blebf = e(nc.sbuf_tensor("blebf", [1, 1024], BF16))
        eyeb = e(nc.sbuf_tensor("eyeb", [128, 128], BF16))
        eyef = e(nc.sbuf_tensor("eyef", [8, 8], F32))
        onesb = e(nc.sbuf_tensor("onesb", [1, 128], BF16))
        onesf = e(nc.sbuf_tensor("onesf", [1, 128], F32))
        kaS = e(nc.sbuf_tensor("kaS", [128, 8], F32))
        tok0S = e(nc.sbuf_tensor("tok0S", [1, 1024], F32))
        cvecS = e(nc.sbuf_tensor("cvecS", [1, 5], F32))
        cnts = e(nc.sbuf_tensor("cnts", [1, 5], F32))
        lyS = e(nc.sbuf_tensor("lyS", [1, 5], F32))
        scr5 = e(nc.sbuf_tensor("scr5", [1, 5], F32))
        smI = e(nc.sbuf_tensor("smI", [1, 16], F32))
        stat12 = e(nc.sbuf_tensor("stat12", [128, 12], F32))
        mv = e(nc.sbuf_tensor("mv", [128, 2], F32))
        sm1 = e(nc.sbuf_tensor("sm1", [128, 14], F32))
        filt = e(nc.sbuf_tensor("filt", [128, 80], F32))
        bitsf = e(nc.sbuf_tensor("bitsf", [128, 24], F32))
        bitsb = e(nc.sbuf_tensor("bitsb", [128, 24], BF16))
        corrf = e(nc.sbuf_tensor("corrf", [128, 24], F32))
        corrb = e(nc.sbuf_tensor("corrb", [128, 24], BF16))
        syn01 = e(nc.sbuf_tensor("syn01", [128, 12], F32))
        g8 = [e(nc.sbuf_tensor(f"g8_{j}", [128, 8], F32)) for j in range(2)]
        g8b = [e(nc.sbuf_tensor(f"g8b{j}", [128, 8], BF16))
               for j in range(2)]
        gFbf = [e(nc.sbuf_tensor(f"gFbf{j}", [8, 128], BF16)) for j in range(2)]
        tFbf = e(nc.sbuf_tensor("tFbf", [25, 128], BF16))
        rscp = e(nc.sbuf_tensor("rscp", [8, 4], F32))
        rrow = e(nc.sbuf_tensor("rrow", [1, 8], F32))
        rscB = e(nc.sbuf_tensor("rscB", [128, 8], F32))
        lgt = e(nc.sbuf_tensor("lgt", [128, 49], F32))
        kst = e(nc.sbuf_tensor("kst", [128, 3], F32))
        gsc = e(nc.sbuf_tensor("gsc", [128, 16], F32))

        t0n = x3b[0]
        TTbuf = HFs[1]
        hr2b = HFs[0]
        hrv2b_ofs = 2048
        scrR = x3b[1]
        centS = x2b[1]
        scrC = x2b[0]
        pA0 = e(nc.psum_tensor("pA0", [128, 1024], F32))
        pA1 = e(nc.psum_tensor("pA1", [128, 1024], F32))
        pH = e(nc.psum_tensor("pH", [128, 1024], F32))
        pT = e(nc.psum_tensor("pT", [128, 2048], BF16))
        pA = [pA0, pA1]

        sem_names = ["s_t", "s_v", "s_s", "s_g", "s_x0", "s_x1", "s_w1", "s_w2",
                     "s_sm", "s_b0", "s_b1", "s_tr", "s_tt", "s_st0", "s_st1"]
        sems = {nm: e(nc.semaphore(nm)) for nm in sem_names}
        block = e(nc.Block())

        S = _Sched(sems)
        N_SM = 29 * 16

        base_srcs = []
        for kc in range(8):
            for src in (dcoshi, dcoslo, dsinhi, dsinlo):
                base_srcs.append((src, kc))

        ptb = {"n": 0}

        def pt_batch(titems, evac_fn, t_alias=None, s_alias=None, rhs=None,
                     evac_eng="s"):
            b = ptb["n"]
            ptb["n"] += 1
            ptb.setdefault("eng", {})[b] = evac_eng
            slot = (b % 2) * 1024
            if b >= 2:
                S.wait("t", (ptb["eng"][b - 2], "ptevac", b - 2))

            def tfn(t):
                last = None
                for co, in_ap in titems:
                    width = in_ap.shape[0]
                    if rhs is None:
                        last = t.transpose(
                            pT[0:in_ap.shape[1], slot + co:slot + co + width],
                            in_ap, eyeb[:, :])
                    else:
                        last = t.matmul(
                            pA1[0:in_ap.shape[1],
                                slot // 2 + co:slot // 2 + co + width],
                            in_ap, rhs, start=True, stop=True)
                return last
            S.op("t", tfn, inc=("t", "ptb", b))
            if t_alias is not None:
                S.alias(t_alias, ("t", "ptb", b))
            S.wait(evac_eng, ("t", "ptb", b))
            S.op(evac_eng, lambda e: evac_fn(e, slot),
                 inc=(evac_eng, "ptevac", b))
            if s_alias is not None:
                S.alias(s_alias, (evac_eng, "ptevac", b))

        def walk():
            ptb["n"] = 0
            s1guard = set()
            def g_memsets(g):
                g.memset(onesb[:, :], 1.0)
                g.memset(onesf[:, :], 1.0)
                g.memset(smI[:, 0:1], 2.0)
                g.memset(kst[:, 0:1], 1e-5)
                g.memset(kst[:, 1:2], PI)
                g.memset(gdiag[:, :], 0.0)
                g.memset(tFbf[:, :], 1.0)
                return g.memset(kst[:, 2:3], PI / 2)
            S.op("g", g_memsets, inc=("g", "consts"))

            def g_small(g):
                sm = sems["s_sm"]
                g.dma_start(tok0S[:, :], dtok0).then_inc(sm, 16)
                g.dma_start(kaS[:, :], dka).then_inc(sm, 16)
                g.dma_start(x2b[0][:, 0:16], dfiltc).then_inc(sm, 16)
                g.dma_start(eyef[:, :], deye[0:8, 0:8]).then_inc(sm, 16)
                g.dma_start(eyeb[:, :], deye).then_inc(sm, 16)
                g.dma_start(cvecS[:, :], dcvec).then_inc(sm, 16)
                g.dma_start(Htbf[:, :], dht).then_inc(sm, 16)
                g.dma_start(WleBF[:, :], dwle).then_inc(sm, 16)
                g.dma_start(belbf[:, :], dbel).then_inc(sm, 16)
                g.dma_start(blebf[:, :], dble).then_inc(sm, 16)
                g.dma_start(b1bf[:, :], db1).then_inc(sm, 16)
                g.dma_start(b2bf[:, :], db2).then_inc(sm, 16)
                g.dma_start(centS[0:8, :], dcent).then_inc(sm, 16)
                for kc in range(8):
                    g.dma_start(WelBF[:, kc * 24:(kc + 1) * 24],
                                dwel[kc * 128:(kc + 1) * 128, :]).then_inc(sm, 16)
                for kc in range(8):
                    g.dma_start(CWr[:, kc * 16:(kc + 1) * 16],
                                dcwr[kc * 128:(kc + 1) * 128, :]).then_inc(sm, 16)
            S.op("g", lambda g: g_small(g))

            S.op("y", lambda y: y.dma_start(
                x0b[0][:, :], dx[0:128, :]).then_inc(sems["s_x0"], 16))
            if n_sub > 1:
                S.op("y", lambda y: y.dma_start(
                    x0b[1][:, :], dx[128:256, :]).then_inc(sems["s_x1"], 16))

            S.wait_dma("v", "s_sm", N_SM)
            S.wait("v", ("g", "consts"))

            def v_t0stats(v):
                v.bn_stats(stat12[0:1, 0:6], tok0S[:, 0:512])
                v.bn_stats(stat12[0:1, 6:12], tok0S[:, 512:1024])
                v.drain()
                return v.bn_aggr(smI[0:1, 1:3], stat12[0:1, 0:12])
            S.op("v", v_t0stats, inc=("v", "t0stats"))
            S.wait("s", ("v", "t0stats"))
            S.wait("s", ("g", "consts"))

            def s_rstd0(s):
                s.activation(smI[0:1, 13:14], smI[0:1, 2:3], AF.Ln, bias=kst[0:1, 0:1])
                s.drain()
                return s.activation(smI[0:1, 3:4], smI[0:1, 13:14], AF.Exp, scale=-0.5)
            S.op("s", s_rstd0, inc=("s", "rstd0"))
            S.wait("v", ("s", "rstd0"))

            def v_t0n(v):
                v.tensor_scalar(smI[0:1, 4:5], smI[0:1, 1:2], -1.0, smI[0:1, 3:4],
                                OP.mult, OP.mult)
                v.drain()
                v.tensor_scalar(t0n[0:1, :], tok0S[:, :], smI[0:1, 3:4],
                                smI[0:1, 4:5], OP.mult, OP.add)
                last = None
                for idx, bs in enumerate([4, 16, 64, 256, 1024]):
                    v.reduce_sum(smI[0:1, 5:6], t0n[0:1, 0:bs], axis=AX)
                    v.drain()
                    v.tensor_scalar(smI[0:1, 6:7], smI[0:1, 5:6], 1.0 / bs, None,
                                    OP.mult)
                    v.drain()
                    v.tensor_scalar(scrR[0:1, 0:bs], t0n[0:1, 0:bs], smI[0:1, 6:7],
                                    None, OP.is_gt, OP.add,
                                    accum_out=smI[0:1, 5:6])
                    v.drain()
                    last = v.tensor_scalar(cnts[0:1, idx:idx + 1], smI[0:1, 5:6],
                                           1.0, None, OP.max)
                return last
            S.op("v", v_t0n, inc=("v", "cnts"))
            S.wait("s", ("v", "cnts"))

            def s_ly(s):
                s.activation(lyS[0:1, :], cnts[0:1, :], AF.Ln)
                return s.activation(smI[0:1, 7:8], smI[0:1, 0:1], AF.Ln)
            S.op("s", s_ly, inc=("s", "ly"))
            S.wait("v", ("s", "ly"))

            def v_D(v):
                v.tensor_tensor(scr5[0:1, :], lyS[0:1, :], cvecS[0:1, :], OP.mult)
                v.tensor_scalar(smI[0:1, 9:10], smI[0:1, 7:8], 50.0, None, OP.mult)
                v.drain()
                v.reduce_sum(smI[0:1, 8:9], scr5[0:1, :], axis=AX)
                v.reciprocal(smI[0:1, 10:11], smI[0:1, 9:10])
                v.drain()
                v.tensor_scalar(smI[0:1, 11:12], smI[0:1, 8:9], smI[0:1, 10:11],
                                -1.0, OP.mult, OP.mult)
                v.drain()
                return v.tensor_scalar(smI[0:1, 12:13], smI[0:1, 11:12], 1.0, 2.0,
                                       OP.max, OP.min)
            S.op("v", v_D, inc=("v", "D"))

            S.wait("t", ("v", "D"))
            S.wait("t", ("g", "consts"))
            S.op("t", lambda t: t.matmul(pH[:, 40:41], onesf[:, :], smI[0:1, 12:13],
                                         start=True, stop=True), inc=("t", "Db"))
            S.wait("s", ("t", "Db"))
            S.op("s", lambda s: s.copy(filt[:, 0:1], pH[:, 40:41]), inc=("s", "Db"))

            S.wait("v", ("s", "Db"))

            def v_filt1(v):
                v.tensor_tensor(filt[:, 1:2], filt[:, 0:1], filt[:, 0:1], OP.mult)
                v.tensor_scalar(filt[:, 3:4], filt[:, 0:1], 0.5, 1.0, OP.mult, OP.add)
                v.tensor_scalar(filt[:, 6:7], filt[:, 0:1], C01_HI, None, OP.mult)
                v.drain()
                v.scalar_tensor_tensor(filt[:, 4:5], filt[:, 0:1], C01_LO,
                                       filt[:, 6:7], OP.mult, OP.add)
                v.tensor_scalar(filt[:, 5:6], filt[:, 1:2], C02_HI, None, OP.mult)
                v.drain()
                v.tensor_tensor(filt[:, 7:8], filt[:, 5:6], filt[:, 0:1], OP.add)
                v.drain()
                v.scalar_tensor_tensor(filt[:, 2:3], filt[:, 1:2], C02_LO,
                                       filt[:, 7:8], OP.mult, OP.add)
                v.drain()
                return v.tensor_scalar(filt[:, 8:16], kaS[:, :], filt[:, 3:4],
                                       filt[:, 2:3], OP.mult, OP.add)
            S.op("v", v_filt1, inc=("v", "u1"))

            def v_piarg(v):
                v.drain()
                v.tensor_scalar(filt[:, 56:64], filt[:, 8:16], -1.0, PI_HI,
                                OP.mult, OP.add)
                v.drain()
                return v.tensor_scalar(filt[:, 56:64] if False else filt[:, 64:72],
                                       filt[:, 56:64], PI_LO, None, OP.add)
            S.op("v", v_piarg, inc=("v", "piarg"))
            S.wait("s", ("v", "piarg"))

            def s_real(s):
                s.activation(filt[:, 32:40], filt[:, 64:72], AF.Sin)
                return s.activation(filt[:, 16:24], kaS[:, :], AF.Square)
            S.op("s", s_real, inc=("s", "real"))
            S.wait("v", ("s", "real"))

            def v_phase(v):
                v.scalar_tensor_tensor(filt[:, 8:16], filt[:, 16:24], filt[:, 4:5],
                                       kaS[:, :], OP.mult, OP.subtract)
                v.drain()
                return v.tensor_scalar(filt[:, 24:32], filt[:, 8:16], filt[:, 2:3],
                                       None, OP.add)
            S.op("v", v_phase, inc=("v", "phase"))

            def v_pi2arg(v):
                v.drain()
                v.tensor_scalar(filt[:, 56:64], filt[:, 24:32], -1.0, PI2_HI,
                                OP.mult, OP.add)
                v.drain()
                return v.tensor_scalar(filt[:, 64:72], filt[:, 56:64], PI2_LO,
                                       None, OP.add)
            S.op("v", v_pi2arg, inc=("v", "pi2arg"))
            S.wait("s", ("v", "pi2arg"))

            def s_ph(s):
                s.activation(filt[:, 40:48], filt[:, 24:32], AF.Sin)
                return s.activation(filt[:, 48:56], filt[:, 64:72], AF.Sin)
            S.op("s", s_ph, inc=("s", "ph"))
            S.wait("v", ("s", "ph"))

            def v_fr(v):
                v.tensor_scalar(filt[:, 64:72], filt[:, 32:40], 1.0 / 1024, None,
                                OP.mult)
                v.drain()
                v.tensor_tensor(filt[:, 56:64], filt[:, 64:72], filt[:, 48:56],
                                OP.mult)
                return v.tensor_tensor(filt[:, 72:80], filt[:, 64:72], filt[:, 40:48],
                                       OP.mult)
            S.op("v", v_fr, inc=("v", "fr"))

            def v_frb(v):
                v.drain()
                v.tensor_tensor(filt[:, 24:32], x2b[0][:, 0:8], filt[:, 56:64],
                                OP.subtract)
                v.tensor_tensor(filt[:, 32:40], x2b[0][:, 8:16], filt[:, 72:80],
                                OP.subtract)
                v.tensor_scalar(filt[:, 5:6], filt[:, 0:1], 1.0, None, OP.is_le)
                v.drain()
                v.tensor_scalar(filt[:, 40:48], filt[:, 24:32], filt[:, 5:6],
                                None, OP.mult)
                v.tensor_scalar(filt[:, 48:56], filt[:, 32:40], filt[:, 5:6],
                                None, OP.mult)
                v.drain()
                v.tensor_tensor(filt[:, 8:16], filt[:, 40:48], filt[:, 56:64],
                                OP.add)
                return v.tensor_tensor(filt[:, 16:24], filt[:, 48:56],
                                       filt[:, 72:80], OP.add)
            S.op("v", v_frb, inc=("v", "frb"))

            for c in range(32):
                if c < 2:
                    S.wait("g", ("v", "cnts"))
                else:
                    S.wait("g", ("t", "base", c - 2))
                src, kc = base_srcs[c]
                S.op("g", lambda g, src=src, kc=kc, c=c: g.dma_start(
                    x3b[c % 2][:, :], src[kc * 128:(kc + 1) * 128, :]).then_inc(
                    sems[f"s_b{c % 2}"], 16))

            for kc in range(8):
                S.op("g", lambda g, kc=kc: g.dma_start(
                    W1bf[:, kc * 4096:(kc + 1) * 4096],
                    dw1[kc * 128:(kc + 1) * 128, :]).then_inc(sems["s_w1"], 16))
            for kc2 in range(32):
                S.op("g", lambda g, kc2=kc2: g.dma_start(
                    W2bf[:, kc2 * 1024:(kc2 + 1) * 1024],
                    dw2[kc2 * 128:(kc2 + 1) * 128, :]).then_inc(sems["s_w2"], 16))

            S.wait("t", ("v", "frb"))
            ncs = {"cos": 0, "sin": 0}
            for c, (src, kc) in enumerate(base_srcs):
                S.wait_dma("t", f"s_b{c % 2}", 16 * (c // 2 + 1))
                is_sin = (c % 4) >= 2
                key = "sin" if is_sin else "cos"
                first = ncs[key] == 0
                lastf = ncs[key] == 15
                ncs[key] += 1
                lhs_col = (16 + kc) if is_sin else (8 + kc)
                dstp = pA1 if is_sin else pA0
                ring = x3b[c % 2]

                def t_base(t, lhs_col=lhs_col, dstp=dstp, first=first, lastf=lastf,
                           ring=ring):
                    t.matmul(dstp[0:1, 0:512], filt[:, lhs_col:lhs_col + 1],
                             ring[:, 0:512], start=first, stop=lastf)
                    return t.matmul(dstp[0:1, 512:1024], filt[:, lhs_col:lhs_col + 1],
                                    ring[:, 512:1024], start=first, stop=lastf)
                S.op("t", t_base, inc=("t", "base", c))

            S.wait("s", ("t", "base", 31))
            S.op("s", lambda s: s.copy(scrR[0:1, 0:1024], pA0[0:1, 0:1024]),
                 inc=("s", "screvac"))
            S.wait("v", ("s", "screvac"))

            def v_hr(v):
                v.tensor_tensor(hr2b[0:1, 0:1024], scrR[0:1, 0:1024], pA1[0:1, 0:1024],
                                OP.subtract)
                v.tensor_copy(hr2b[0:1, 1024:2048], hr2b[0:1, 0:1024])
                v.tensor_tensor(HFs[0][0:1, 2048:3072], scrR[0:1, 0:1024],
                                pA1[0:1, 0:1024], OP.add)
                return v.tensor_copy(HFs[0][0:1, 3072:4096], HFs[0][0:1, 2048:3072])
            S.op("v", v_hr, inc=("v", "hr2b"))

            S.wait("t", ("v", "hr2b"))
            S.wait("t", ("g", "consts"))
            for rnd in range(4):
                srcbuf = HFs[0]
                qofs = rnd * 1024
                if rnd > 0:
                    S.wait("t", ("s", "hrbc", rnd - 1))

                def t_bc(t, srcbuf=srcbuf, qofs=qofs):
                    t.matmul(pA0[:, 0:512], onesb[:, :],
                             srcbuf[0:1, qofs:qofs + 512], start=True, stop=True)
                    return t.matmul(pA0[:, 512:1024], onesb[:, :],
                                    srcbuf[0:1, qofs + 512:qofs + 1024],
                                    start=True, stop=True)
                S.op("t", t_bc, inc=("t", "hrbc", rnd))
                S.wait("s", ("t", "hrbc", rnd))
                S.op("s", lambda s, rnd=rnd: s.copy(
                    HFs[0][:, rnd * 1024:(rnd + 1) * 1024], pA0[:, 0:1024]),
                    inc=("s", "hrbc", rnd))

            S.wait("y", ("s", "hrbc", 3))
            S.op("y", lambda y: y.dma_start(
                TTbuf[:, 0:1024],
                bass.AP(HFs[0], 2048 + 128, [[4095, 128], [128, 8], [1, 128]])
            ).then_inc(sems["s_tt"], 16))
            S.op("y", lambda y: y.dma_start(
                Tring[:, 0:1024],
                bass.AP(HFs[0], 1024, [[4095, 128], [128, 8], [1, 128]])
            ).then_inc(sems["s_tr"], 16))
            S.op("s", lambda s: s.dma_start(
                Tring[:, 1024:1408],
                bass.AP(HFs[0], 1024, [[4095, 128], [128, 3], [1, 128]])
            ).then_inc(sems["s_tr"], 16))

            S.wait_dma("t", "s_tt", 16)
            S.wait_dma("t", "s_sm", N_SM)
            for kc in range(8):
                if kc > 0:
                    S.wait("t", ("s", "cw", kc - 1))

                def t_cw(t, kc=kc):
                    last = None
                    for mc in range(8):
                        dtile = (mc - kc) % 8
                        last = t.matmul(pH[:, 64:88],
                                        TTbuf[:, (7 - dtile) * 128:(8 - dtile) * 128],
                                        WelBF[:, mc * 24:(mc + 1) * 24],
                                        start=(mc == 0), stop=(mc == 7))
                    return last
                S.op("t", t_cw, inc=("t", "cw", kc))
                S.wait("s", ("t", "cw", kc))
                S.op("s", lambda s, kc=kc: s.copy(CWbf[:, kc * 24:(kc + 1) * 24],
                                                  pH[:, 64:88]),
                     inc=("s", "cw", kc))

            def v_ssC(v):
                return v.scalar_tensor_tensor(scrC[0:8, :], centS[0:8, :], 0.0,
                                              centS[0:8, :], OP.bypass, OP.mult,
                                              accum_out=rscp[:, 0:1])
            S.op("v", v_ssC, inc=("v", "ssC"))
            S.wait("s", ("v", "ssC"))

            def s_rsc(s):
                s.activation(rscp[:, 1:2], rscp[:, 0:1], AF.Ln)
                s.drain()
                return s.activation(rscp[:, 3:4], rscp[:, 1:2], AF.Exp, scale=0.5)
            S.op("s", s_rsc, inc=("s", "rscn"))
            S.wait("v", ("s", "rscn"))

            def v_rsc(v):
                v.tensor_scalar(rscp[:, 2:3], rscp[:, 3:4], 1e-12, None, OP.max)
                v.drain()
                return v.reciprocal(rscp[:, 0:1], rscp[:, 2:3])
            S.op("v", v_rsc, inc=("v", "rsc"))
            S.wait("t", ("v", "rsc"))
            S.op("t", lambda t: t.transpose(pH[0:1, 48:56], rscp[0:8, 0:1],
                                            eyef[0:8, 0:8]), inc=("t", "rscT"))
            S.wait("s", ("t", "rscT"))
            S.op("s", lambda s: s.copy(rrow[0:1, :], pH[0:1, 48:56]),
                 inc=("s", "rrow"))
            S.wait("t", ("s", "rrow"))
            S.op("t", lambda t: t.matmul(pH[:, 48:56], onesf[:, :], rrow[0:1, :],
                                         start=True, stop=True), inc=("t", "rscBmm"))
            S.wait("s", ("t", "rscBmm"))
            S.op("s", lambda s: s.copy(rscB[:, :], pH[:, 48:56]), inc=("s", "rscB"))

            def st_x0load(i):
                par = i % 2
                if i >= 2:
                    S.wait("y", ("v", "x2", i - 2))
                    S.op("y", lambda y, i=i, par=par: y.dma_start(
                        x0b[par][:, :], dx[i * 128:(i + 1) * 128, :]).then_inc(
                        sems[f"s_x{par}"], 16))

            def st_ln1(i):
                par = i % 2
                S.wait_dma("v", f"s_x{par}", 16 * (i // 2 + 1))
                if i >= 2:
                    S.wait("v", ("t", "yFtr", i - 2))

                def v_ln1(v, par=par):
                    v.bn_stats(stat12[:, 0:6], x0b[par][:, 0:512])
                    v.bn_stats(stat12[:, 6:12], x0b[par][:, 512:1024])
                    v.drain()
                    return v.bn_aggr(mv[:, 0:2], stat12[:, :])
                S.op("v", v_ln1, inc=("v", "stats1", i))
                S.wait("s", ("v", "stats1", i))

                def s_rstd1(s):
                    s.activation(sm1[:, 10:11], mv[:, 1:2], AF.Ln, bias=kst[:, 0:1])
                    s.drain()
                    return s.activation(sm1[:, 0:1], sm1[:, 10:11], AF.Exp,
                                        scale=-0.5)
                S.op("s", s_rstd1, inc=("s", "rstd1", i))
                S.wait("v", ("s", "rstd1", i))

                def v_u(v, par=par):
                    v.tensor_scalar(sm1[:, 1:2], mv[:, 0:1], -1.0, sm1[:, 0:1],
                                    OP.mult, OP.mult)
                    v.drain()
                    return v.tensor_scalar(bfA[par][:, :], x0b[par][:, :],
                                           sm1[:, 0:1], sm1[:, 1:2], OP.mult, OP.add)
                S.op("v", v_u, inc=("v", "u", i))

            def st_uFtr(i):
                par = i % 2
                S.wait("t", ("v", "u", i))
                pt_batch([(j * 128, bfA[par][:, j * 128:(j + 1) * 128])
                          for j in range(8)],
                         lambda s, slot, par=par: s.copy(bfF[par][:, :],
                                                         pT[:, slot:slot + 1024]),
                         t_alias=("t", "uFtr", i), s_alias=("s", "uFevac", i))

            def st_x1(i):
                par = i % 2
                S.wait("t", ("s", "uFevac", i))
                S.wait_dma("t", "s_tr", 32)
                S.wait_dma("t", "s_tt", 16)
                if i >= 2:
                    S.wait("t", ("v", "x3", i - 2))

                def t_x1(t, par=par):
                    last = None
                    for h in range(2):
                        for kc in range(8):
                            j0 = ((h * 4 - kc) % 8) * 128
                            last = t.matmul(pA[par][:, h * 512:(h + 1) * 512],
                                            bfF[par][:, kc * 128:(kc + 1) * 128],
                                            Tring[:, j0:j0 + 512],
                                            start=(kc == 0), stop=False)
                    return last
                S.op("t", t_x1, inc=("t", "x1", i))

            def st_lat(i):
                par = i % 2
                g_ = i // 2
                lofs = 0 if par == 0 else 512
                S.wait("t", ("s", "cw", 7))
                if i >= 2:
                    S.wait("t", ("v", "bits", i - 2))
                if g_ >= 1:
                    S.wait("t", ("s", "gelu", (2 * (g_ - 1) + 1, 6 + par)))

                def t_lat(t, par=par, lofs=lofs):
                    for kc in range(8):
                        t.matmul(pH[:, lofs:lofs + 24],
                                 bfF[par][:, kc * 128:(kc + 1) * 128],
                                 CWbf[:, kc * 24:(kc + 1) * 24],
                                 start=(kc == 0), stop=False)
                    return t.matmul(pH[:, lofs:lofs + 24], onesb[:, :],
                                    belbf[0:1, :], start=False, stop=True)
                S.op("t", t_lat, inc=("t", "lat", i))

            def st_bits(i):
                par = i % 2
                lofs = 0 if par == 0 else 512
                S.wait("v", ("t", "lat", i))

                def v_bits(v, lofs=lofs):
                    v.tensor_scalar(bitsf[:, :], pH[:, lofs:lofs + 24], 0.0, None,
                                    OP.is_gt)
                    v.drain()
                    return v.tensor_copy(bitsb[:, :], bitsf[:, :])
                S.op("v", v_bits, inc=("v", "bits", i))

            def st_bitsF(i):
                S.wait("t", ("v", "bits", i))
                pt_batch([(0, bitsb[:, 0:24])],
                         lambda s, slot: s.copy(tFbf[0:24, :],
                                                pT[0:24, slot:slot + 128]),
                         t_alias=("t", "bitsFtr", i), s_alias=("s", "bitsFev", i))

            def st_syn(i):
                par = i % 2
                sofs = 128 if par == 0 else 640
                S.wait("t", ("s", "bitsFev", i))
                if i >= 2:
                    S.wait("t", ("v", "corr", i - 2))

                def t_syn(t, sofs=sofs):
                    return t.matmul(pH[:, sofs:sofs + 12], tFbf[0:24, :],
                                    Htbf[0:24, :], start=True, stop=True)
                S.op("t", t_syn, inc=("t", "syn", i))

            def st_corr(i):
                par = i % 2
                sofs = 128 if par == 0 else 640
                S.wait("v", ("t", "syn", i))

                def v_corr(v, sofs=sofs):
                    v.tensor_scalar(syn01[:, :], pH[:, sofs:sofs + 12], 0.5, None,
                                    OP.is_gt)
                    v.tensor_scalar(gsc[:, 0:12], bitsf[:, 0:12], -2.0, 1.0,
                                    OP.mult, OP.add)
                    v.drain()
                    v.reduce_sum(gsc[:, 12:13], syn01[:, :], axis=AX)
                    v.drain()
                    v.tensor_scalar(gsc[:, 13:14], gsc[:, 12:13], 0.5, None,
                                    OP.is_ge)
                    v.tensor_scalar(gsc[:, 14:15], gsc[:, 12:13], 3.5, None,
                                    OP.is_le)
                    v.drain()
                    v.tensor_tensor(gsc[:, 15:16], gsc[:, 13:14], gsc[:, 14:15],
                                    OP.mult)
                    v.drain()
                    v.tensor_scalar(corrf[:, 12:24], syn01[:, :], gsc[:, 15:16],
                                    None, OP.mult)
                    v.drain()
                    v.scalar_tensor_tensor(corrf[:, 0:12], gsc[:, 0:12], 0.0,
                                           corrf[:, 12:24], OP.bypass, OP.mult)
                    v.drain()
                    v.tensor_tensor(gsc[:, 0:12], corrf[:, 0:12], bitsf[:, 0:12],
                                    OP.add)
                    v.drain()
                    v.tensor_copy(corrf[:, 0:12], gsc[:, 0:12])
                    v.tensor_copy(corrf[:, 12:24], bitsf[:, 12:24])
                    v.drain()
                    return v.tensor_copy(corrb[:, :], corrf[:, :])
                S.op("v", v_corr, inc=("v", "corr", i))

            def st_wle(i):
                par = i % 2
                S.wait("t", ("v", "corr", i))
                pt_batch([(0, corrb[:, 0:24])],
                         lambda s, slot: s.copy(tFbf[0:24, :],
                                                pT[0:24, slot:slot + 128]),
                         t_alias=("t", "corrFtr", i), s_alias=("s", "corrFev", i))
                S.wait("t", ("s", "corrFev", i))

                def t_wle(t, par=par):
                    t.matmul(pA[par][:, 0:512], tFbf[0:25, :], WleBF[:, 0:512],
                             start=False, stop=True)
                    return t.matmul(pA[par][:, 512:1024], tFbf[0:25, :],
                                    WleBF[:, 512:1024], start=False, stop=True)
                S.op("t", t_wle, inc=("t", "attn", i))

            def st_x2ln2(i):
                par = i % 2
                S.wait("v", ("t", "attn", i))
                S.op("v", lambda v, par=par: v.scalar_tensor_tensor(
                    x2b[par][:, :], x0b[par][:, :], 0.0, pA[par][:, :],
                    OP.bypass, OP.add),
                    inc=("v", "x2", i))

                def v_ln2(v, par=par):
                    v.bn_stats(stat12[:, 0:6], x2b[par][:, 0:512])
                    v.bn_stats(stat12[:, 6:12], x2b[par][:, 512:1024])
                    v.drain()
                    return v.bn_aggr(sm1[:, 6:8], stat12[:, :])
                S.op("v", v_ln2, inc=("v", "stats2", i))
                S.wait("s", ("v", "stats2", i))

                def s_rstd2(s):
                    s.activation(sm1[:, 10:11], sm1[:, 7:8], AF.Ln, bias=kst[:, 0:1])
                    s.activation(sm1[:, 11:12], sm1[:, 7:8], AF.Ln, scale=1024.0)
                    s.drain()
                    s.activation(sm1[:, 5:6], sm1[:, 10:11], AF.Exp, scale=-0.5)
                    return s.activation(sm1[:, 8:9], sm1[:, 11:12], AF.Exp,
                                        scale=0.5)
                S.op("s", s_rstd2, inc=("s", "rstd2", i))
                S.wait("v", ("s", "rstd2", i))
                S.wait("v", ("t", "uFtr", i))

                def v_y(v, par=par):
                    v.tensor_scalar(sm1[:, 2:3], sm1[:, 6:7], -1.0, sm1[:, 5:6],
                                    OP.mult, OP.mult)
                    v.tensor_tensor(sm1[:, 9:10], sm1[:, 8:9], sm1[:, 5:6],
                                    OP.mult)
                    v.drain()
                    return v.tensor_scalar(bfA[par][:, :], x2b[par][:, :],
                                           sm1[:, 5:6], sm1[:, 2:3], OP.mult, OP.add)
                S.op("v", v_y, inc=("v", "y", i))

                def v_rn(v, par=par):
                    v.tensor_scalar(sm1[:, 3:4], sm1[:, 9:10], 1e-12, None, OP.max)
                    v.drain()
                    return v.reciprocal(sm1[:, 12 + par:13 + par], sm1[:, 3:4])
                S.op("v", v_rn, inc=("v", "rn", i))

            def st_yFtr(i):
                par = i % 2
                S.wait("t", ("v", "y", i))
                pt_batch([(j * 128, bfA[par][:, j * 128:(j + 1) * 128])
                          for j in range(8)],
                         lambda s, slot, par=par: s.copy(bfF[par][:, :],
                                                         pT[:, slot:slot + 1024]),
                         t_alias=("t", "yFtr", i), s_alias=("s", "yFevac", i))

            def st_gating(i):
                par = i % 2
                gofs = 240 if par == 0 else 752
                S.wait("t", ("s", "yFevac", i))
                S.wait("t", ("v", "x2", i + 1 if par == 0 else i))
                if i >= 2:
                    S.wait("t", ("v", "gbf", i - 2))
                    S.wait("t", ("v", "x3", i - 1 if par == 0 else i - 2))
                else:
                    S.wait("t", ("s", "hrbc", 3))

                def t_gat(t, par=par, gofs=gofs):
                    last = None
                    for kc in range(8):
                        last = t.matmul(pA1[:, gofs:gofs + 16],
                                        bfF[par][:, kc * 128:(kc + 1) * 128],
                                        CWr[:, kc * 16:(kc + 1) * 16],
                                        start=(kc == 0), stop=(kc == 7))
                    return last
                S.op("t", t_gat, inc=("t", "gat", i))

            def st_gd8(i):
                par = i % 2
                if i >= 1:
                    S.wait("v", ("t", "hftr", (i - 1, 7)))

                def v_gd8(v, par=par):
                    last = None
                    for oc in range(8):
                        last = v.tensor_scalar(
                            gdiag[:, oc * 128:(oc + 1) * 128], eyeb[:, :],
                            g8[par][:, oc:oc + 1], None, OP.mult)
                    return last
                S.op("v", v_gd8, inc=("v", "gd8", i))

            def st_gates(i):
                par = i % 2
                gofs = 240 if par == 0 else 752
                S.wait("v", ("t", "gat", i))
                S.wait("v", ("s", "rscB"))
                S.wait("v", ("v", "rn", i)) if False else None

                def v_gate1(v, gofs=gofs, par=par):
                    v.tensor_tensor(lgt[:, 32:40], pA1[:, gofs:gofs + 8],
                                    rscB[:, :], OP.mult)
                    v.drain()
                    v.scalar_tensor_tensor(lgt[:, 0:8], lgt[:, 32:40],
                                           sm1[:, 12 + par:13 + par],
                                           pA1[:, gofs + 8:gofs + 16], OP.mult,
                                           OP.add)
                    v.drain()
                    v.reduce_max(lgt[:, 40:41], lgt[:, 0:8], axis=AX)
                    v.reduce_max(lgt[:, 41:42], lgt[:, 0:8], axis=AX, negate=True)
                    v.drain()
                    v.tensor_scalar(lgt[:, 8:16], lgt[:, 0:8], lgt[:, 40:41], None,
                                    OP.is_ge, OP.add, accum_out=lgt[:, 42:43])
                    v.drain()
                    v.scalar_tensor_tensor(lgt[:, 16:24], lgt[:, 8:16], -1e30,
                                           lgt[:, 0:8], OP.mult, OP.add)
                    v.tensor_scalar(lgt[:, 44:45], lgt[:, 42:43], 1.5, None,
                                    OP.is_ge)
                    v.drain()
                    v.reduce_max(lgt[:, 43:44], lgt[:, 16:24], axis=AX)
                    v.drain()
                    v.tensor_tensor(lgt[:, 45:46], lgt[:, 40:41], lgt[:, 43:44],
                                    OP.subtract)
                    v.drain()
                    v.tensor_tensor(lgt[:, 46:47], lgt[:, 44:45], lgt[:, 45:46],
                                    OP.mult)
                    v.drain()
                    v.tensor_tensor(lgt[:, 45:46], lgt[:, 46:47], lgt[:, 43:44],
                                    OP.add)
                    v.drain()
                    v.tensor_scalar(lgt[:, 48:49], lgt[:, 41:42], 0.5, None,
                                    OP.mult)
                    return v.tensor_scalar(lgt[:, 8:16], lgt[:, 0:8],
                                           lgt[:, 45:46], None, OP.is_ge)
                S.op("v", v_gate1, inc=("v", "mask", i))
                S.wait("s", ("v", "mask", i))
                S.op("s", lambda s: s.activation(lgt[:, 24:32], lgt[:, 0:8],
                                                 AF.Tanh, scale=0.5,
                                                 bias=lgt[:, 48:49]),
                     inc=("s", "E", i))
                S.wait("v", ("s", "E", i))

                def v_gate2(v, par=par):
                    v.tensor_scalar(lgt[:, 16:24], lgt[:, 24:32], 1.0, None,
                                    OP.add)
                    v.tensor_scalar(lgt[:, 0:8], lgt[:, 24:32], -1.0, 1.0,
                                    OP.mult, OP.add)
                    v.drain()
                    v.reciprocal(lgt[:, 24:32], lgt[:, 0:8])
                    v.drain()
                    v.tensor_tensor(lgt[:, 0:8], lgt[:, 16:24], lgt[:, 24:32],
                                    OP.mult)
                    v.drain()
                    v.scalar_tensor_tensor(lgt[:, 32:40], lgt[:, 0:8], 0.0,
                                           lgt[:, 8:16], OP.bypass, OP.mult,
                                           accum_out=lgt[:, 46:47])
                    v.drain()
                    v.reciprocal(lgt[:, 47:48], lgt[:, 46:47])
                    v.drain()
                    v.tensor_scalar(g8[par][:, :], lgt[:, 32:40], lgt[:, 47:48],
                                    None, OP.mult)
                    v.drain()
                    return v.tensor_copy(g8b[par][:, :], g8[par][:, :])
                S.op("v", v_gate2, inc=("v", "gbf", i))

                if par == 0:
                    st_gd8(i)

            def st_gFtr(i):
                par = i % 2
                S.wait("t", ("v", "gbf", i))
                pt_batch([(0, g8b[par][:, 0:8])],
                         lambda s, slot, par=par: s.copy(gFbf[par][:, :],
                                                         pT[0:8, slot:slot + 128]),
                         t_alias=("t", "gFtr", i), s_alias=("s", "gFev", i))

            def ffn_ops(g_, ops):
                i0, i1 = 2 * g_, 2 * g_ + 1
                chunks = [(i0, oc) for oc in range(8)] + \
                         [(i1, oc) for oc in range(8)]
                hps = [pA[0][:, 0:512], pA[0][:, 512:1024],
                       pH[:, 0:512], pH[:, 512:1024]]
                for kind, idx_c in ops:
                    if kind == "gd8":
                        st_gd8(i1)
                        continue
                    i2, oc = chunks[idx_c]
                    p2 = i2 % 2
                    buf = idx_c % 4
                    if kind == "H":
                        S.wait("t", ("s", "yFevac", i2))
                        if idx_c >= 4:
                            S.wait("t", ("s", "gelu", chunks[idx_c - 4]))
                        if idx_c == 0:
                            S.wait_dma("t", "s_w1", 16 * 8)
                            S.wait("t", ("v", "x2", i0))
                        if g_ == 0 and idx_c == 2:
                            S.wait("t", ("s", "rscB"))

                        def t_H(t, p2=p2, oc=oc, buf=buf):
                            for kc in range(8):
                                t.matmul(hps[buf],
                                         bfF[p2][:, kc * 128:(kc + 1) * 128],
                                         W1bf[:, kc * 4096 + oc * 512:
                                              kc * 4096 + (oc + 1) * 512],
                                         start=(kc == 0), stop=False)
                            return t.matmul(hps[buf], onesb[:, :],
                                            b1bf[0:1, oc * 512:(oc + 1) * 512],
                                            start=False, stop=True)
                        S.op("t", t_H, inc=("t", "H", (i2, oc)))

                        S.wait("s", ("t", "H", (i2, oc)))
                        if idx_c >= 4:
                            S.wait("s", ("t", "hftr", chunks[idx_c - 4]))
                        if g_ == 0 and idx_c == 0:
                            S.wait_dma("s", "s_tr", 32)

                        def s_gelu(s, buf=buf):
                            return s.activation(Htmp[buf][:, :], hps[buf], AF.Gelu)
                        S.op("s", s_gelu, inc=("s", "gelu", (i2, oc)))

                    else:
                        S.wait("t", ("s", "gelu", (i2, oc)))
                        S.wait("t", ("v", "gd8", i2))
                        if (ptb["n"] % 2) == 1 and g_ not in s1guard:
                            s1guard.add(g_)
                            S.wait("t", ("v", "mask", 2 * g_ + 1))
                        pt_batch([(j * 128,
                                   Htmp[buf][:, j * 128:(j + 1) * 128])
                                  for j in range(4)],
                                 lambda e, slot, i2=i2, oc=oc: e.tensor_copy(
                                     HFs[i2 % 2][:, oc * 512:(oc + 1) * 512],
                                     pA1[:, slot // 2:slot // 2 + 512]),
                                 t_alias=("t", "hftr", (i2, oc)),
                                 s_alias=("s", "hfev", (i2, oc)),
                                 rhs=gdiag[:, oc * 128:(oc + 1) * 128],
                                 evac_eng="v")

            def st_w2_one(g_, i2):
                i0, i1 = 2 * g_, 2 * g_ + 1
                chunks = [(i0, oc) for oc in range(8)] + \
                         [(i1, oc) for oc in range(8)]
                if True:
                    p2 = i2 % 2
                    for oc in range(8):
                        S.wait("t", ("s", "hfev", (i2, oc)))
                    S.wait("t", ("v", "x2", i2))
                    S.wait("t", ("s", "gFev", i2))
                    if i2 == i0:
                        S.wait("t", ("s", "gelu", chunks[12]))
                        S.wait("t", ("s", "gelu", chunks[13]))
                    S.wait_dma("t", "s_w2", 16 * 32)

                    def t_w2(t, p2=p2):
                        last = None
                        for dc in range(2):
                            for kc2 in range(32):
                                last = t.matmul(
                                    pA[p2][:, dc * 512:(dc + 1) * 512],
                                    HFs[p2][:, kc2 * 128:(kc2 + 1) * 128],
                                    W2bf[:, kc2 * 1024 + dc * 512:
                                         kc2 * 1024 + (dc + 1) * 512],
                                    start=(kc2 == 0), stop=False)
                            last = t.matmul(pA[p2][:, dc * 512:(dc + 1) * 512],
                                            gFbf[p2][0:8, :],
                                            b2bf[0:8, dc * 512:(dc + 1) * 512],
                                            start=False, stop=True)
                        return last
                    S.op("t", t_w2, inc=("t", "W2", i2))

                    S.wait("v", ("t", "W2", i2))
                    if i2 >= 2:
                        S.wait_dma("v", f"s_st{i2 % 2}", 16 * (i2 // 2))

                    def v_x3(v, p2=p2):
                        return v.tensor_tensor(x3b[p2][:, :], x2b[p2][:, :],
                                               pA[p2][:, :], OP.add)
                    S.op("v", v_x3, inc=("v", "x3", i2))

                    S.wait("s", ("v", "x3", i2))
                    srcb = x3b if _STAGE == "full" else x2b
                    S.op("s", lambda s, i2=i2, srcb=srcb: s.dma_start(
                        do[i2 * 128:(i2 + 1) * 128, :],
                        srcb[i2 % 2][:, :]).then_inc(sems[f"s_st{i2 % 2}"], 16))

            def st_w2_part(g_, i2, part):
                i0 = 2 * g_
                chunks = [(i0, oc) for oc in range(8)] + \
                         [(i0 + 1, oc) for oc in range(8)]
                p2 = i2 % 2
                dc = part // 2
                k0 = (part % 2) * 16
                if part == 0:
                    for oc in range(8):
                        S.wait("t", ("s", "hfev", (i2, oc)))
                    S.wait("t", ("v", "x2", i2))
                    S.wait("t", ("s", "gFev", i2))
                    S.wait_dma("t", "s_w2", 16 * 32)

                def t_w2p(t, p2=p2, dc=dc, k0=k0, part=part):
                    last = None
                    for kc2 in range(k0, k0 + 16):
                        last = t.matmul(
                            pA[p2][:, dc * 512:(dc + 1) * 512],
                            HFs[p2][:, kc2 * 128:(kc2 + 1) * 128],
                            W2bf[:, kc2 * 1024 + dc * 512:
                                 kc2 * 1024 + (dc + 1) * 512],
                            start=(k0 == 0 and kc2 == k0), stop=False)
                    if part % 2 == 1:
                        last = t.matmul(pA[p2][:, dc * 512:(dc + 1) * 512],
                                        gFbf[p2][0:8, :],
                                        b2bf[0:8, dc * 512:(dc + 1) * 512],
                                        start=False, stop=True)
                    return last
                S.op("t", t_w2p, inc=("t", "W2p", (i2, part)))

            def st_w2_fin(g_, i2):
                p2 = i2 % 2
                S.wait("v", ("t", "W2p", (i2, 3)))
                if i2 >= 2:
                    S.wait_dma("v", f"s_st{i2 % 2}", 16 * (i2 // 2))

                def v_x3(v, p2=p2):
                    return v.tensor_tensor(x3b[p2][:, :], x2b[p2][:, :],
                                           pA[p2][:, :], OP.add)
                S.op("v", v_x3, inc=("v", "x3", i2))

                S.wait("s", ("v", "x3", i2))
                srcb = x3b if _STAGE == "full" else x2b
                S.op("s", lambda s, i2=i2, srcb=srcb: s.dma_start(
                    do[i2 * 128:(i2 + 1) * 128, :],
                    srcb[i2 % 2][:, :]).then_inc(sems[f"s_st{i2 % 2}"], 16))

            n_grp = n_sub // 2
            st_ln1(0)
            if n_sub > 1:
                st_ln1(1)
            for g_ in range(n_grp):
                i0, i1 = 2 * g_, 2 * g_ + 1
                st_uFtr(i0)
                st_x1(i0)
                st_uFtr(i1)
                st_lat(i0)
                st_bits(i0)
                st_x1(i1)
                st_bitsF(i0)
                st_lat(i1)
                st_syn(i0)
                st_corr(i0)
                st_bits(i1)
                st_wle(i0)
                st_bitsF(i1)
                st_x2ln2(i0)
                st_syn(i1)
                st_corr(i1)
                st_yFtr(i0)
                st_wle(i1)
                st_x2ln2(i1)
                st_gating(i0)
                ffn_ops(g_, [("H", 0), ("H", 1)])
                st_gates(i0)
                ffn_ops(g_, [("H", 2), ("H", 3)])
                st_yFtr(i1)
                st_gating(i1)
                st_gates(i1)
                ffn_ops(g_, [("H", 4), ("hftr", 0), ("H", 5), ("hftr", 1)])
                st_gFtr(i0)
                st_gFtr(i1)
                seq = [("H", 6), ("hftr", 2), ("H", 7), ("hftr", 3)]
                for idx in range(8, 16):
                    seq += [("H", idx), ("hftr", idx - 4)]
                    if idx == 11:
                        seq += [("gd8", 0)]
                seq += [("hftr", 12), ("hftr", 13), ("hftr", 14), ("hftr", 15)]
                ffn_ops(g_, seq)
                st_w2_one(g_, i0)
                if g_ + 1 < n_grp:
                    st_x0load(i0 + 2)
                    st_x0load(i1 + 2)
                    st_ln1(i0 + 2)
                    st_ln1(i1 + 2)
                st_w2_one(g_, i1)

            n_st0 = (n_sub + 1) // 2
            n_st1 = n_sub // 2
            S.wait_dma("s", "s_st0", 16 * n_st0)
            S.wait_dma("s", "s_st1", 16 * n_st1)

        S.emit_eng = None
        walk()

        @block.tensor
        def _(t):
            S.begin("t", t)
            walk()

        @block.vector
        def _(v):
            S.begin("v", v)
            walk()

        @block.scalar
        def _(s):
            S.begin("s", s)
            walk()

        @block.gpsimd
        def _(g):
            S.begin("g", g)
            walk()

        @block.sync
        def _(y):
            S.begin("y", y)
            walk()

    return nc


def _shard_inputs(L0_attn_x, L0_attn_W_el, L0_attn_b_el, L0_attn_W_le,
                  L0_attn_b_le, L0_ffn_w1, L0_ffn_b1, L0_ffn_w2, L0_ffn_b2,
                  L0_ffn_centroids, L0_ffn_w_route, n_sub=N_SUB_FULL):
    C = _consts()
    f32 = np.float32
    x = np.ascontiguousarray(np.asarray(L0_attn_x, dtype=f32).reshape(-1, 1024))
    tok0 = np.ascontiguousarray(x[0:1, :])
    wel = np.ascontiguousarray(np.asarray(L0_attn_W_el, dtype=f32))
    bel = np.ascontiguousarray(np.asarray(L0_attn_b_el, dtype=f32).reshape(1, 24))
    ble = np.ascontiguousarray(np.asarray(L0_attn_b_le, dtype=f32).reshape(1, 1024))
    wle = np.ascontiguousarray(np.concatenate(
        [np.asarray(L0_attn_W_le, dtype=f32), ble], axis=0))
    w1 = np.asarray(L0_ffn_w1, dtype=f32)
    w1r = np.ascontiguousarray(w1.transpose(1, 0, 2).reshape(1024, 4096))
    b1r = np.ascontiguousarray(np.asarray(L0_ffn_b1, dtype=f32).reshape(1, 4096))
    w2r = np.ascontiguousarray(np.asarray(L0_ffn_w2, dtype=f32).reshape(4096, 1024))
    b2r = np.ascontiguousarray(np.asarray(L0_ffn_b2, dtype=f32))
    cent = np.ascontiguousarray(np.asarray(L0_ffn_centroids, dtype=f32))
    wrt = np.asarray(L0_ffn_w_route, dtype=f32)
    cwr = np.ascontiguousarray(
        np.concatenate([cent.T, wrt.T], axis=1))
    shared = dict(tok0=tok0, wel=wel, bel=bel, wle=wle, ble=ble, w1r=w1r,
                  b1r=b1r, w2r=w2r, b2r=b2r, cwr=cwr, cent=cent,
                  kaF=C["kaF"], filtc=C["filtc"], Ht=C["Ht"], eye=C["eye"],
                  cvec=C["cvec"],
                  coshi=C["coshi"], coslo=C["coslo"], sinhi=C["sinhi"],
                  sinlo=C["sinlo"])
    rows = n_sub * 128
    in_maps = []
    for c in range(N_CORES):
        m = dict(shared)
        m["x"] = x[c * rows:(c + 1) * rows, :]
        in_maps.append(m)
    return in_maps


def kernel(L0_attn_x, L0_attn_W_el, L0_attn_b_el, L0_attn_W_le, L0_attn_b_le,
           L0_ffn_w1, L0_ffn_b1, L0_ffn_w2, L0_ffn_b2, L0_ffn_centroids,
           L0_ffn_w_route):
    in_maps = _shard_inputs(L0_attn_x, L0_attn_W_el, L0_attn_b_el, L0_attn_W_le,
                            L0_attn_b_le, L0_ffn_w1, L0_ffn_b1, L0_ffn_w2,
                            L0_ffn_b2, L0_ffn_centroids, L0_ffn_w_route)
    nc = _build(N_SUB_FULL)
    res = run_bass_kernel_spmd(nc, in_maps, core_ids=list(range(N_CORES)))
    out = np.concatenate([np.asarray(r["o"]) for r in res.results], axis=0)
    return out.reshape(16, 2048, 1024)
